# version 23
# baseline (speedup 1.0000x reference)
import numpy as np
import ml_dtypes
import concourse.bass as bass
import concourse.mybir as mybir
from concourse.bass_utils import run_bass_kernel_spmd
from concourse.alu_op_type import AluOpType
from contextlib import ExitStack

F32 = mybir.dt.float32
BF16 = mybir.dt.bfloat16
AF = mybir.ActivationFunctionType
AX = mybir.AxisListType
BF = ml_dtypes.bfloat16
MULT = AluOpType.mult
ADD = AluOpType.add
SUB = AluOpType.subtract

T = 4096
D = 1024
EPS = 1e-5
ISC_O = 1.0 / 128.0
ISC_I = float(1.0 / np.sqrt(32.0))

DEBUG_TAPS = []
DEBUG_HEAD = 0

ENGINES = ("tensor", "scalar", "vector", "gpsimd", "sync")

PERM32 = np.concatenate([np.arange(0, 32, 2), np.arange(1, 32, 2)])


def host_constants():
    c = {}
    pos = np.arange(T, dtype=np.float64)[None, :]
    f_o = np.power(10000.0, -np.arange(64, dtype=np.float64) / 64.0)[:, None]
    ang = f_o * pos
    c["ropeA_o"] = np.concatenate([np.cos(ang), np.cos(ang)], 0).astype(np.float32)
    c["ropeB_o"] = np.concatenate([-np.sin(ang), np.sin(ang)], 0).astype(np.float32)
    inv = 1.0 / 10000.0 ** (np.arange(0, 32, 2, dtype=np.float64) / 32.0)
    ang = inv[:, None] * pos
    A1 = np.concatenate([np.cos(ang), np.cos(ang)], 0)
    B1 = np.concatenate([-np.sin(ang), np.sin(ang)], 0)
    c["ropeA_i"] = np.tile(A1, (4, 1)).astype(np.float32)
    c["ropeB_i"] = np.tile(B1, (4, 1)).astype(np.float32)
    p = np.arange(128)[:, None]
    col = np.arange(512)[None, :]
    masks = np.stack([(p <= col - 128 * j).astype(np.float32) for j in range(4)])
    c["masks"] = masks.astype(BF)
    sw = np.zeros((128, 128), np.float32)
    for i in range(64):
        sw[i + 64, i] = 1.0
        sw[i, i + 64] = 1.0
    c["swapO"] = sw.astype(BF)
    sw = np.zeros((128, 128), np.float32)
    for s in range(0, 128, 32):
        for i in range(16):
            sw[s + 16 + i, s + i] = 1.0
            sw[s + i, s + 16 + i] = 1.0
    c["swapI"] = sw.astype(BF)
    c["identF"] = np.eye(128, dtype=np.float32)
    c["identB"] = np.eye(128, dtype=np.float32).astype(BF)
    c["onesB"] = np.ones((128, 128), np.float32).astype(BF)
    c["onesF"] = np.ones((128, 128), np.float32)
    ep = np.zeros((8, 1024), np.float32)
    for e in range(8):
        ep[e, e * 128:(e + 1) * 128] = 1.0
    c["epat"] = ep.astype(BF)
    c["epsc"] = np.full((128, 1), 1e-5, np.float32)
    return c


def host_weights(w):
    r = {}
    r["wqr"] = np.ascontiguousarray(w["L0_attn_wq"].reshape(8, 8, 128, 128).transpose(0, 2, 1, 3))
    r["wkr"] = np.ascontiguousarray(w["L0_attn_wk"].reshape(8, 8, 128, 128).transpose(0, 2, 1, 3))
    r["wvr"] = np.ascontiguousarray(w["L0_attn_wv"].reshape(8, 8, 128, 128).transpose(0, 2, 1, 3))
    r["wor"] = np.ascontiguousarray(w["L0_attn_wo"])
    r["wdownr"] = np.ascontiguousarray(w["L0_ffn_w_down"].reshape(8, 128, 4, 128).transpose(1, 0, 2, 3))
    wq_p = w["L0_ffn_wq"].reshape(512, 16, 32)[:, :, PERM32].reshape(512, 512)
    wk_p = w["L0_ffn_wk"].reshape(512, 16, 32)[:, :, PERM32].reshape(512, 512)
    r["wqir"] = np.ascontiguousarray(wq_p.reshape(4, 128, 4, 128).transpose(1, 0, 2, 3))
    r["wkir"] = np.ascontiguousarray(wk_p.reshape(4, 128, 4, 128).transpose(1, 0, 2, 3))
    r["wvir"] = np.ascontiguousarray(w["L0_ffn_wv"].reshape(4, 128, 4, 128).transpose(1, 0, 2, 3))
    r["wupr"] = np.ascontiguousarray(w["L0_ffn_w_up"].reshape(4, 128, 8, 128).transpose(1, 0, 2, 3))
    r["routerr"] = np.ascontiguousarray(w["L0_ffn_router_w"].reshape(8, 128, 8).transpose(1, 0, 2))
    r["w1r"] = np.ascontiguousarray(w["L0_ffn_w1"].reshape(8, 8, 128, 16, 128).transpose(0, 3, 2, 1, 4))
    r["w3r"] = np.ascontiguousarray(w["L0_ffn_w3"].reshape(8, 8, 128, 16, 128).transpose(0, 3, 2, 1, 4))
    r["w2r"] = np.ascontiguousarray(w["L0_ffn_w2"].reshape(8, 16, 128, 1024).transpose(0, 2, 1, 3))
    r["g1fm"] = np.ascontiguousarray(w["L0_ffn_g1"].reshape(8, 128).T)
    r["g2fm"] = np.ascontiguousarray(w["L0_ffn_g2"].reshape(8, 128).T)
    return r


class Sem:
    def __init__(self, nc, stack, name):
        self.h = stack.enter_context(nc.semaphore(name))
        self.val = 0


class Builder:
    def __init__(self, nc, st):
        self.nc = nc
        self.ops = {e: [] for e in ENGINES}
        self.sem = {e: Sem(nc, st, "sem_" + e) for e in ENGINES}
        self.sem["gdma"] = Sem(nc, st, "sem_gdma")
        for nm in ("dTA0", "dTA1", "dTB0", "dTB1", "gW13", "gW2", "gTS", "gOB"):
            self.sem[nm] = Sem(nc, st, "sem_" + nm)
        self.stacks = []

    def push(self):
        self.stacks.append(ExitStack())

    def pop(self):
        self.stacks.pop().close()

    def sbuf(self, name, shape, dtype):
        return self.stacks[-1].enter_context(self.nc.sbuf_tensor(name, shape, dtype))

    def psum(self, name, shape, dtype=F32):
        return self.stacks[-1].enter_context(self.nc.psum_tensor(name, shape, dtype))

    def op(self, eng, fn, wait=None):
        s = self.sem[eng]
        inc_n = 16 if eng == "sync" else 1
        s.val += inc_n
        tgt = s.val
        wl = [w for w in (wait or []) if w is not None]
        if eng in ("scalar", "vector"):
            wl.insert(0, (s, tgt - inc_n))

        def closure(e, fn=fn, wl=wl, inc_n=inc_n, s=s):
            for (ws, wv) in wl:
                e.wait_ge(ws.h, wv)
            fn(e).then_inc(s.h, inc_n)
        self.ops[eng].append(closure)
        return (s, tgt)

    def ld(self, out_ap, in_ap, wait=None):
        return self.op("sync", lambda e, o=out_ap, i=in_ap: e.dma_start(out=o, in_=i), wait)

    def ld_on(self, semname, out_ap, in_ap, wait=None):
        s = self.sem[semname]
        s.val += 16
        tgt = s.val
        wl = [w for w in (wait or []) if w is not None]
        wl.insert(0, (s, tgt - 16))

        def closure(e, o=out_ap, i=in_ap, wl=wl, s=s):
            for (ws, wv) in wl:
                e.wait_ge(ws.h, wv)
            e.dma_start(out=o, in_=i).then_inc(s.h, 16)
        self.ops["sync"].append(closure)
        return (s, tgt)

    def gld(self, out_ap, in_ap, wait=None):
        s = self.sem["gdma"]
        s.val += 16
        tgt = s.val
        wl = [w for w in (wait or []) if w is not None]

        def closure(e, o=out_ap, i=in_ap, wl=wl, s=s):
            for (ws, wv) in wl:
                e.wait_ge(ws.h, wv)
            e.dma_start(out=o, in_=i).then_inc(s.h, 16)
        self.ops["gpsimd"].append(closure)
        return (s, tgt)

    def gld_on(self, semname, out_ap, in_ap, wait=None):
        s = self.sem[semname]
        s.val += 16
        tgt = s.val
        wl = [w for w in (wait or []) if w is not None]

        def closure(e, o=out_ap, i=in_ap, wl=wl, s=s):
            for (ws, wv) in wl:
                e.wait_ge(ws.h, wv)
            e.dma_start(out=o, in_=i).then_inc(s.h, 16)
        self.ops["gpsimd"].append(closure)
        return (s, tgt)

    def gop(self, fn, wait=None):
        s = self.sem["gpsimd"]
        s.val += 1
        tgt = s.val
        wl = [w for w in (wait or []) if w is not None]
        wl.insert(0, (s, tgt - 1))

        def closure(e, fn=fn, wl=wl, s=s):
            for (ws, wv) in wl:
                e.wait_ge(ws.h, wv)
            fn(e).then_inc(s.h, 1)
        self.ops["gpsimd"].append(closure)
        return (s, tgt)

    def barrier(self, skip_gw=False):
        targets = {}
        for e in ENGINES:
            s = self.sem[e]
            inc_n = 16 if e == "sync" else 1
            own_prev = s.val
            s.val += inc_n
            targets[e] = (s.val, own_prev, inc_n)
        for e in ENGINES:
            s = self.sem[e]
            tgt, own_prev, inc_n = targets[e]

            gdma_val = self.sem["gdma"].val
            dvals = {nm: self.sem[nm].val for nm in ("dTA0", "dTA1", "dTB0", "dTB1")}
            gvals = ({} if skip_gw else
                     {nm: self.sem[nm].val for nm in ("gW13", "gW2")})

            def closure(eng, e=e, s=s, own_prev=own_prev, inc_n=inc_n,
                        targets=dict(targets), gdma_val=gdma_val, dvals=dvals):
                if e in ("sync", "gpsimd"):
                    eng.wait_ge(s.h, own_prev)
                if e == "gpsimd":
                    eng.wait_ge(self.sem["gdma"].h, gdma_val)
                    for nm, v in gvals.items():
                        eng.wait_ge(self.sem[nm].h, v)
                if e == "sync":
                    for nm, v in dvals.items():
                        eng.wait_ge(self.sem[nm].h, v)
                if e == "sync":
                    eng.nop().then_inc(s.h, inc_n)
                else:
                    eng.drain().then_inc(s.h, inc_n)
                for o in ENGINES:
                    if o != e:
                        eng.wait_ge(self.sem[o].h, targets[o][0])
            self.ops[e].append(closure)


def cpt(ap, c):
    return ap.rearrange("p (c t) -> p c t", c=c)


def build_program(taps=(), upto=9):
    nc = bass.Bass("TRN2", target_bir_lowering=False)
    st = ExitStack()
    b = Builder(nc, st)

    xh = nc.dram_tensor("x", [T, D], F32, kind="ExternalInput").ap()
    outh = nc.dram_tensor("out", [T, D], F32, kind="ExternalOutput").ap()

    din = {}
    for nm, shape, dt in [
        ("ropeA_o", [128, T], F32), ("ropeB_o", [128, T], F32),
        ("ropeA_i", [128, T], F32), ("ropeB_i", [128, T], F32),
        ("masks", [4, 128, 512], BF16),
        ("swapO", [128, 128], BF16), ("swapI", [128, 128], BF16),
        ("identB", [128, 128], BF16), ("identF", [128, 128], F32),
        ("onesB", [128, 128], BF16), ("onesF", [128, 128], F32),
        ("epat", [8, 1024], BF16), ("epsc", [128, 1], F32),
        ("wqr", [8, 128, 8, 128], F32), ("wkr", [8, 128, 8, 128], F32),
        ("wvr", [8, 128, 8, 128], F32), ("wor", [8, 128, 1024], F32),
        ("wdownr", [128, 8, 4, 128], F32),
        ("wqir", [128, 4, 4, 128], F32), ("wkir", [128, 4, 4, 128], F32),
        ("wvir", [128, 4, 4, 128], F32), ("wupr", [128, 4, 8, 128], F32),
        ("routerr", [128, 8, 8], F32),
        ("w1r", [8, 16, 128, 8, 128], F32), ("w3r", [8, 16, 128, 8, 128], F32),
        ("w2r", [8, 128, 16, 1024], F32),
        ("g1fm", [128, 8], F32), ("g2fm", [128, 8], F32),
    ]:
        din[nm] = nc.dram_tensor(nm, shape, dt, kind="ExternalInput").ap()

    def scratch(nm, shape, dt):
        kind = "ExternalOutput" if nm in taps else "Internal"
        return nc.dram_tensor(nm, shape, dt, kind=kind).ap()

    xT_s = scratch("xT_s", [8, 128, T], F32)
    x1_s = scratch("x1_s", [8, 128, T], F32)
    ln3_s = scratch("ln3_s", [8, 128, T], F32)
    xf_s = scratch("xf_s", [8, 128, T], F32)
    h_s = scratch("h_s", [8, 128, T], BF16)
    t_s = scratch("t_s", [8, 128, T], BF16)
    ob_s = scratch("ob_s", [8, 128, T], BF16)
    c_s = scratch("c_s", [4, 128, T], BF16)
    o2_s = scratch("o2_s", [4, 128, T], BF16)
    gate_s = scratch("gate_s", [8, T], BF16)
    y_s = scratch("y_s", [8, 128, T], F32)
    qkv_s = scratch("qkv_s", [3, 128, T], BF16)

    b.push()
    C_swapO = b.sbuf("c_swapO", [128, 128], BF16)
    C_swapI = b.sbuf("c_swapI", [128, 128], BF16)
    C_identB = b.sbuf("c_identB", [128, 128], BF16)
    C_identF = b.sbuf("c_identF", [128, 128], F32)
    C_onesB = b.sbuf("c_onesB", [128, 128], BF16)
    C_onesF = b.sbuf("c_onesF", [128, 128], F32)
    C_epat = b.sbuf("c_epat", [8, 1024], BF16)
    C_eps = b.sbuf("c_eps", [128, 1], F32)
    C_g1 = b.sbuf("c_g1", [128, 8], F32)
    C_g2 = b.sbuf("c_g2", [128, 8], F32)
    GATE = b.sbuf("gate_fm", [8, T], BF16)
    SROW = b.sbuf("srow", [1, 1536], F32)
    STAT = b.sbuf("stat", [128, 16], F32)
    CBIAS = b.sbuf("cbias", [128, 8], F32)
    RECROW = b.sbuf("recrow", [128, 512], F32)
    BCAST = b.sbuf("bcastt", [128, 512], F32)

    for nm, dst in [("swapO", C_swapO),
                    ("swapI", C_swapI), ("identB", C_identB), ("identF", C_identF),
                    ("onesB", C_onesB), ("onesF", C_onesF), ("epat", C_epat),
                    ("epsc", C_eps),
                    ("g1fm", C_g1), ("g2fm", C_g2)]:
        if nm in ("masks", "masks4"):
            b.ld(dst[:].rearrange("p (j c) -> p j c", j=4),
                 din[nm].rearrange("j p c -> p j c"))
        else:
            b.ld(dst[:], din[nm])
    b.barrier()

    b.push()
    TABA = b.sbuf("taba", [128, 1024], F32)
    TABB = b.sbuf("tabb", [128, 1024], F32)
    C_masks = b.sbuf("c_masks", [128, 2048], BF16)
    b.ld(C_masks[:].rearrange("p (j c) -> p j c", j=4),
         din["masks"].rearrange("j p c -> p j c"))
    b.push()
    b.push()
    LN1 = b.sbuf("ln1", [128, 8 * T], BF16)

    b.push()
    P1X = b.sbuf("p1_x", [128, 2048], F32)
    P1XC = b.sbuf("p1_xc", [128, 1024], F32)
    P1SQ = b.sbuf("p1_sq", [128, 1024], F32)
    P1LN = b.sbuf("p1_lnt", [128, 1024], F32)
    P1ST = b.sbuf("p1_xstg", [128, 2048], F32)
    PS_L1 = b.psum("p1_psl", [128, 1024], F32)
    PS_X1 = b.psum("p1_psx", [128, 1024], F32)

    x_guard = [None, None]
    st_dma = [None, None]
    prev = {"sq": None, "tp": None, "evl": None, "evx": None, "rstd": None}
    for it in range(32):
        sl = it % 2
        xsl = P1X[:, 1024 * sl:1024 * (sl + 1)]
        t_load = b.ld_on(f"dTB{sl}", xsl, xh[128 * it:128 * (it + 1), :],
                         wait=[x_guard[sl]])
        b.op("vector", lambda e, xsl=xsl: e.tensor_reduce(
            STAT[:, 0:1], xsl, axis=AX.X, op=ADD), wait=[t_load])
        b.op("vector", lambda e: e.tensor_scalar(
            STAT[:, 1:2], STAT[:, 0:1], 1.0 / D, None, op0=MULT))
        t_xc = b.op("vector", lambda e, xsl=xsl: e.tensor_scalar(
            P1XC[:], xsl, STAT[:, 1:2], None, op0=SUB), wait=[prev["sq"]])
        t_sq = b.op("scalar", lambda e: e.activation(
            P1SQ[:], P1XC[:], AF.Square, accum_out=STAT[:, 2:3]), wait=[t_xc])
        t_std = b.op("scalar", lambda e: e.activation(
            STAT[:, 3:4], STAT[:, 2:3], AF.Sqrt, bias=C_eps[:, 0:1],
            scale=1.0 / D), wait=[prev["rstd"]])
        t_rstd = b.op("vector", lambda e: e.reciprocal(
            STAT[:, 4:5], STAT[:, 3:4]), wait=[t_std])
        t_ln = b.op("vector", lambda e: e.tensor_scalar(
            P1LN[:], P1XC[:], STAT[:, 4:5], None, op0=MULT), wait=[prev["tp"]])
        prev["sq"] = t_sq
        prev["rstd"] = t_rstd
        tp = None
        for c in range(8):
            wv = [t_ln] if c == 0 else None
            if c == 0 and prev["evl"] is not None:
                wv = [t_ln, prev["evl"]]
            tp = b.op("tensor", lambda e, c=c: e.transpose(
                PS_L1[:, 128 * c:128 * (c + 1)], P1LN[:, 128 * c:128 * (c + 1)],
                C_identF[:]), wait=wv)
        prev["tp"] = tp
        tpx = None
        for c in range(8):
            wv = None
            if c == 0 and prev["evx"] is not None:
                wv = [prev["evx"]]
            tpx = b.op("tensor", lambda e, c=c, xsl=xsl: e.transpose(
                PS_X1[:, 128 * c:128 * (c + 1)], xsl[:, 128 * c:128 * (c + 1)],
                C_identF[:]), wait=wv)
        x_guard[sl] = tpx
        t_evl = b.op("scalar", lambda e, it=it: e.copy(
            cpt(LN1[:], 8)[:, :, 128 * it:128 * (it + 1)], cpt(PS_L1[:], 8)),
            wait=[tp])
        prev["evl"] = t_evl
        stg = P1ST[:, 1024 * sl:1024 * (sl + 1)]
        t_evx = b.op("scalar", lambda e, stg=stg: e.copy(stg, PS_X1[:]),
                     wait=[tpx, st_dma[sl]])
        prev["evx"] = t_evx
        st_dma[sl] = b.ld_on(f"dTA{sl}",
                             xT_s[:, :, 128 * it:128 * (it + 1)]
                             .rearrange("c p t -> p c t"),
                             cpt(stg, 8), wait=[t_evx])
    b.barrier()
    b.pop()

    if upto < 2:
        return finish(nc, st, b)

    b.push()
    QKV2 = b.sbuf("p2_qkv", [128, 6 * T], BF16)
    WS2 = b.sbuf("p2_ws", [128, 6144], BF16)
    SQB2 = b.sbuf("p2_sqb", [128, 512], BF16)
    TMPG = b.sbuf("p2_tmpg", [128, 1024], F32)
    TMPB = b.sbuf("p2_tmpb", [128, 1024], F32)
    QSW = b.sbuf("p2_qsw", [128, 2 * T], BF16)
    PBUF = b.sbuf("p2_pbuf", [128, 2048], BF16)
    OSTH = b.sbuf("p2_osth", [128, 1024], BF16)
    PS_P2 = b.psum("p2_proj", [128, 1024], F32)
    PS_W2 = b.psum("p2_swap", [128, 512], F32)
    PS_S2 = b.psum("p2_S", [128, 2048], F32)
    PS_O2 = b.psum("p2_O", [128, 512], F32)
    PS_L2 = PS_W2
    PS_N2 = PS_W2

    g2s = {"bcev": None, "cbias": None, "osg": None,
           "pguard": [None, None], "lguard": None, "exphist": [None, None],
           "odma": [None, None], "swev": None, "bvlast": None}

    def cauchy_norms(buf, region_off, statcol, init=None, fins=None):
        last = init
        for tt in range(8):
            wvs = [last]
            if fins is not None:
                wvs.append(fins[tt])
            t_sq = b.op("scalar", lambda e, ro=region_off, tt=tt, buf=buf:
                        e.activation(
                SQB2[:], buf[:, ro + 512 * tt:ro + 512 * tt + 512], AF.Square),
                wait=wvs)
            wvm = [t_sq, last]
            if tt == 0:
                wvm.append(g2s["bcev"])
            t_mm = b.op("tensor", lambda e: e.matmul(
                PS_N2[0:1, :], C_onesB[:, 0:1], SQB2[:], start=True, stop=True),
                wait=wvm)
            t_mx = b.op("vector", lambda e: e.reduce_max(
                STAT[0:1, 14:15], PS_N2[0:1, :], axis=AX.X), wait=[t_mm])
            if tt == 0:
                last = b.op("vector", lambda e, sc=statcol: e.tensor_copy(
                    STAT[0:1, sc:sc + 1], STAT[0:1, 14:15]),
                    wait=[g2s["cbias"]])
            else:
                last = b.op("vector", lambda e, sc=statcol: e.tensor_tensor(
                    STAT[0:1, sc:sc + 1], STAT[0:1, sc:sc + 1], STAT[0:1, 14:15],
                    op=AluOpType.max))
        return last

    def bias_from_norms(psN, scale, biascol, wait):
        t1 = b.op("scalar", lambda e: e.activation(
            STAT[0:1, 10:11], STAT[0:1, 8:9], AF.Sqrt), wait=wait)
        t2 = b.op("scalar", lambda e: e.activation(
            STAT[0:1, 11:12], STAT[0:1, 9:10], AF.Sqrt))
        t3 = b.op("vector", lambda e: e.tensor_tensor(
            STAT[0:1, 12:13], STAT[0:1, 10:11], STAT[0:1, 11:12], op=MULT),
            wait=[t1, t2])
        t4 = b.op("vector", lambda e, s=scale: e.tensor_scalar(
            STAT[0:1, 13:14], STAT[0:1, 12:13], -float(s), None, op0=MULT))
        t5 = b.op("tensor", lambda e, psN=psN: e.matmul(
            psN[:, 0:1], C_onesF[0:1, :], STAT[0:1, 13:14], start=True, stop=True),
            wait=[t4, g2s["bcev"]])
        t6 = b.op("scalar", lambda e, bc=biascol, psN=psN: e.copy(
            CBIAS[:, bc:bc + 1], psN[:, 0:1]), wait=[t5])
        g2s["cbias"] = t6
        return t6

    rope_state = {"s0": None, "s1": None, "g0": None, "g1": None}

    def rope_combine(buf, reg, swbuf, swreg, tabA_d, tabB_d, nchunk_tiles,
                     sw_evs, tmpG, tmpB, evs=None):
        prevs = [rope_state["s0"], rope_state["s1"]]
        prevg = [rope_state["g0"], rope_state["g1"]]
        fins = []
        for tt in range(nchunk_tiles):
            sl = tt % 2
            tA = b.ld_on(f"dTA{sl}", TABA[:, 512 * sl:512 * (sl + 1)],
                         tabA_d[:, 512 * (tt % 8):512 * (tt % 8) + 512],
                         wait=[prevs[sl]])
            tB = b.ld_on(f"dTB{sl}", TABB[:, 512 * sl:512 * (sl + 1)],
                         tabB_d[:, 512 * (tt % 8):512 * (tt % 8) + 512],
                         wait=[prevs[sl]])
            sl_ap = buf[:, reg + 512 * tt:reg + 512 * tt + 512]
            sw_ap = swbuf[:, swreg + 512 * tt:swreg + 512 * tt + 512]
            t_a = b.gop(lambda e, s=sl_ap, sl=sl, tmpG=tmpG: e.tensor_tensor(
                tmpG[:, 512 * sl:512 * (sl + 1)], s,
                TABA[:, 512 * sl:512 * (sl + 1)], op=MULT),
                wait=[tA, prevs[sl],
                      evs[tt] if evs is not None else None])
            t_bv = b.op("vector", lambda e, sw=sw_ap, sl=sl, tmpB=tmpB:
                        e.tensor_tensor(
                tmpB[:, 512 * sl:512 * (sl + 1)], sw,
                TABB[:, 512 * sl:512 * (sl + 1)], op=MULT),
                wait=[sw_evs[tt], tB, prevg[sl]])
            t_fin = b.op("vector", lambda e, s=sl_ap, sl=sl, tmpG=tmpG,
                         tmpB=tmpB: e.tensor_tensor(
                s, tmpG[:, 512 * sl:512 * (sl + 1)],
                tmpB[:, 512 * sl:512 * (sl + 1)], op=ADD), wait=[t_a])
            prevs[sl] = t_fin
            prevg[sl] = t_fin
            fins.append(t_fin)
        rope_state["s0"], rope_state["s1"] = prevs
        rope_state["g0"], rope_state["g1"] = prevg
        return fins

    wsem = ["gW13", "gW2"]
    w_tgt = [None, None]
    w_slot_free = [None, None]

    def load_head(h):
        slot = h % 2
        base = 3072 * slot
        sm = wsem[slot]
        t = b.gld_on(sm, WS2[:, base:base + 1024],
                     din["wqr"][h].rearrange("p c f -> p (c f)"),
                     wait=[w_slot_free[slot]])
        t = b.gld_on(sm, WS2[:, base + 1024:base + 2048],
                     din["wkr"][h].rearrange("p c f -> p (c f)"))
        t = b.gld_on(sm, WS2[:, base + 2048:base + 3072],
                     din["wvr"][h].rearrange("p c f -> p (c f)"))
        w_tgt[slot] = t

    hs = {}

    def emit_proj(h):
        slot = h % 2
        wb0 = 3072 * slot
        qb = 3 * T * slot
        st = {"qk_evs": {0: [None] * 8, T: [None] * 8}, "v_evs": [None] * 32}
        prev_ev2 = list(hs[h - 1]["pev"]) if h >= 1 else [None, None]
        first = True
        free_ev = hs[h - 2]["slot_free"] if h >= 2 else None
        for (reg, wb) in ((0, wb0), (T, wb0 + 1024)):
            for tt in range(8):
                par = tt % 2
                psb = PS_P2[:, 512 * par:512 * (par + 1)]
                mm = None
                for c in range(8):
                    wv = None
                    if c == 0:
                        wv = [prev_ev2[par]]
                        if first:
                            wv.append((b.sem[wsem[slot]], w_tgt[slot][1]))
                            first = False
                    mm = b.op("tensor", lambda e, c=c, tt=tt, wb=wb, psb=psb: e.matmul(
                        psb, WS2[:, wb + 128 * c:wb + 128 * (c + 1)],
                        LN1[:, T * c + 512 * tt:T * c + 512 * tt + 512],
                        start=(c == 0), stop=(c == 7)), wait=wv)
                wv = [mm]
                if reg == 0 and tt == 0:
                    wv.append(free_ev)
                ev = b.op("scalar", lambda e, reg=reg, tt=tt, psb=psb, qb=qb: e.copy(
                    QKV2[:, qb + reg + 512 * tt:qb + reg + 512 * tt + 512], psb),
                    wait=wv)
                st["qk_evs"][reg][tt] = ev
                prev_ev2[par] = ev
        lastVmm = None
        for tc in range(32):
            par = tc % 2
            psbv = PS_P2[:, 512 * par:512 * par + 128]
            mm = None
            for c in range(8):
                mm = b.op("tensor", lambda e, c=c, tc=tc, psbv=psbv, wb0=wb0: e.matmul(
                    psbv,
                    LN1[:, T * c + 128 * tc:T * c + 128 * (tc + 1)],
                    WS2[:, wb0 + 2048 + 128 * c:wb0 + 2048 + 128 * (c + 1)],
                    start=(c == 0), stop=(c == 7)),
                    wait=[prev_ev2[par]] if c == 0 else None)
            if tc == 31:
                lastVmm = mm
            ev = b.op("scalar", lambda e, tc=tc, psbv=psbv, qb=qb: e.copy(
                QKV2[:, qb + 2 * T + 128 * tc:qb + 2 * T + 128 * (tc + 1)], psbv),
                wait=[mm])
            st["v_evs"][tc] = ev
            prev_ev2[par] = ev
        swA = PS_W2[:]
        sw_evs = {0: [None] * 8, T: [None] * 8}
        swev = g2s["swev"]
        first_sw = True
        for (reg, _) in ((0, wb0), (T, wb0 + 1024)):
            for tt in range(8):
                wv = [st["qk_evs"][reg][tt], swev]
                if first_sw:
                    wv += [g2s["bvlast"], g2s["cbias"]]
                    first_sw = False
                t_sw = b.op("tensor", lambda e, reg=reg, tt=tt, qb=qb, swA=swA:
                            e.matmul(
                    swA, C_swapO[:],
                    QKV2[:, qb + reg + 512 * tt:qb + reg + 512 * tt + 512],
                    start=True, stop=True), wait=wv)
                swev = b.op("scalar", lambda e, reg=reg, tt=tt, swA=swA: e.copy(
                    QSW[:, reg + 512 * tt:reg + 512 * tt + 512], swA),
                    wait=[t_sw])
                sw_evs[reg][tt] = swev
        g2s["swev"] = swev
        st["wfree"] = lastVmm
        st["v_done"] = st["v_evs"][31]
        st["pev"] = list(prev_ev2)
        st["sw_evs"] = sw_evs
        hs[h] = st

    def emit_rope(h):
        slot = h % 2
        qb = 3 * T * slot
        st = hs[h]
        st["finsq"] = rope_combine(QKV2, qb, QSW, 0, din["ropeA_o"],
                                   din["ropeB_o"], 8, st["sw_evs"][0],
                                   TMPG, TMPB, evs=st["qk_evs"][0])
        st["finsk"] = rope_combine(QKV2, qb + T, QSW, T, din["ropeA_o"],
                                   din["ropeB_o"], 8, st["sw_evs"][T],
                                   TMPG, TMPB, evs=st["qk_evs"][T])
        g2s["bvlast"] = st["finsk"][7]

    def emit_bound(h):
        slot = h % 2
        qb = 3 * T * slot
        st = hs[h]
        tqn = cauchy_norms(QKV2, qb, 8, fins=st["finsq"])
        tkn = cauchy_norms(QKV2, qb + T, 9, init=tqn, fins=st["finsk"])
        st["bias"] = bias_from_norms(PS_N2, ISC_O, slot, [tqn, tkn])

    def emit_attn(h):
        slot = h % 2
        qb = 3 * T * slot
        st = hs[h]
        p_guard = g2s["pguard"]
        ps_o_guard = g2s["osg"]
        ps_l_guard = g2s["lguard"]
        exp_hist = g2s["exphist"]
        prev_odma = g2s["odma"]
        last_pv = None
        for qt in range(8):
            nkc = 4 * qt + 4
            npair = nkc // 2
            last_l = None
            tps = {}
            for p in range(npair + 1):
                if p < npair:
                    psl = p % 2
                    sb = p % 2
                    mms = None
                    for half in range(2):
                        kc = 2 * p + half
                        wv = None
                        if half == 0:
                            wv = [exp_hist[sb]]
                            if p == 0:
                                wv += [st["finsq"][qt], st["finsk"][qt]]
                                if qt == 0:
                                    wv.append(st["v_done"])
                        mms = b.op("tensor", lambda e, kc=kc, qt=qt, sb=sb,
                                   half=half, qb=qb: e.matmul(
                            PS_S2[:, 1024 * sb + 512 * half:
                                  1024 * sb + 512 * (half + 1)],
                            QKV2[:, qb + T + 128 * kc:qb + T + 128 * (kc + 1)],
                            QKV2[:, qb + 512 * qt:qb + 512 * (qt + 1)],
                            start=True, stop=True), wait=wv)
                    t_exp = b.op("scalar", lambda e, psl=psl, sb=sb, slot=slot:
                                 e.activation(
                        PBUF[:, 1024 * psl:1024 * (psl + 1)],
                        PS_S2[:, 1024 * sb:1024 * (sb + 1)], AF.Exp,
                        bias=CBIAS[:, slot:slot + 1], scale=float(ISC_O)),
                        wait=[mms, p_guard[psl], st["bias"]])
                    exp_hist[sb] = t_exp
                    t_p = t_exp
                    for half in range(2):
                        kc = 2 * p + half
                        j = kc - 4 * qt
                        if j >= 0:
                            t_p = b.op("vector", lambda e, psl=psl, j=j,
                                       half=half: e.tensor_tensor(
                                PBUF[:, 1024 * psl + 512 * half:
                                     1024 * psl + 512 * (half + 1)],
                                PBUF[:, 1024 * psl + 512 * half:
                                     1024 * psl + 512 * (half + 1)],
                                C_masks[:, 512 * j:512 * (j + 1)], op=MULT),
                                wait=[t_exp])
                    tps[p] = t_p
                pp = p - 1
                if pp >= 0:
                    psl = pp % 2
                    for half in range(2):
                        kc = 2 * pp + half
                        wv = [tps[pp]] if half == 0 else None
                        if kc == 0:
                            wv = [tps[pp], ps_o_guard]
                        last_pv = b.op("tensor", lambda e, kc=kc, psl=psl,
                                       half=half, nkc=nkc, qb=qb:
                                       e.matmul(
                            PS_O2[:],
                            QKV2[:, qb + 2 * T + 128 * kc:qb + 2 * T + 128 * (kc + 1)],
                            PBUF[:, 1024 * psl + 512 * half:
                                 1024 * psl + 512 * (half + 1)],
                            start=(kc == 0), stop=(kc == nkc - 1)), wait=wv)
                        wv2 = [ps_l_guard] if kc == 0 else None
                        last_l = b.op("tensor", lambda e, kc=kc, psl=psl,
                                      half=half, nkc=nkc: e.matmul(
                            PS_L2[0:1, :], C_onesB[:, 0:1],
                            PBUF[:, 1024 * psl + 512 * half:
                                 1024 * psl + 512 * (half + 1)],
                            start=(kc == 0), stop=(kc == nkc - 1)), wait=wv2)
                    p_guard[psl] = last_l
            t_rec = b.op("vector", lambda e: e.reciprocal(
                RECROW[0:1, :], PS_L2[0:1, :]), wait=[last_l])
            ps_l_guard = None
            t_bc = b.op("tensor", lambda e: e.matmul(
                PS_N2[:], C_onesF[0:1, :], RECROW[0:1, :], start=True, stop=True),
                wait=[t_rec, g2s["cbias"], g2s["swev"]])
            t_bcev = b.op("scalar", lambda e: e.copy(BCAST[:], PS_N2[:]),
                          wait=[t_bc, ps_o_guard])
            g2s["bcev"] = t_bcev
            ps_l_guard = t_bcev
            osl = qt % 2
            ps_o_guard = b.op("vector", lambda e, osl=osl: e.tensor_tensor(
                OSTH[:, 512 * osl:512 * (osl + 1)],
                PS_O2[:], BCAST[:], op=MULT),
                wait=[t_bcev, last_pv, prev_odma[osl]])
            prev_odma[osl] = b.ld(ob_s[h][:, 512 * qt:512 * qt + 512],
                                  OSTH[:, 512 * osl:512 * (osl + 1)],
                                  wait=[ps_o_guard])
        g2s["osg"] = ps_o_guard
        g2s["pguard"] = p_guard
        g2s["lguard"] = ps_l_guard
        g2s["exphist"] = exp_hist
        g2s["odma"] = prev_odma
        st["slot_free"] = last_pv

    load_head(0)
    for h in range(9):
        if h < 8:
            if h >= 1:
                w_slot_free[(h + 1) % 2] = hs[h - 1]["wfree"]
            if h < 7:
                load_head(h + 1)
            emit_proj(h)
        if h < 8:
            emit_rope(h)
        if h == 0:
            emit_bound(0)
        if h >= 1:
            emit_attn(h - 1)
            if h < 8:
                emit_bound(h)
    b.barrier()
    if "qkv_s" in taps:
        for r in range(3):
            b.ld(qkv_s[r], QKV2[:, 3 * T + T * r:3 * T + T * (r + 1)])
        b.gld(y_s[0], QSW[:, 0:T])
        b.gld(y_s[1], QSW[:, T:2 * T])
        b.barrier()
    b.pop()
    b.pop()

    if upto < 3:
        return finish(nc, st, b)

    b.push()
    WO3 = b.sbuf("p3_wo", [128, 8192], BF16)
    OB3 = b.sbuf("p3_ob", [128, 8192], BF16)
    FPA = b.sbuf("p3_fpa", [128, 4096], F32)
    FPD = b.sbuf("p3_fpd", [128, 4096], F32)
    FPB = b.sbuf("p3_fpb", [128, 4096], F32)
    FPC = b.sbuf("p3_fpc", [128, 4096], F32)
    XRES = b.sbuf("p3_xres", [128, 4096], F32)
    BQ3 = b.sbuf("p3_bq", [128, 4096], BF16)
    PS_A3 = b.psum("p3_a", [128, 512], F32)
    PS_M3 = b.psum("p3_m", [128, 512], F32)
    PS_B3 = b.psum("p3_b", [128, 512], F32)

    for hh in range(8):
        b.gld(WO3[:, 1024 * hh:1024 * (hh + 1)], din["wor"][hh])
    b.barrier()

    def stats_mean(src_bf, psM, psB, wait):
        mm = None
        for m in range(8):
            mm = b.op("tensor", lambda e, m=m, psM=psM: e.matmul(
                psM[0:1, :], C_onesB[:, 0:1], src_bf[:, 512 * m:512 * (m + 1)],
                start=(m == 0), stop=(m == 7)), wait=wait if m == 0 else None)
        t_r = b.op("vector", lambda e, psM=psM: e.tensor_scalar(
            SROW[0:1, 512:1024], psM[0:1, :], 1.0 / D, None, op0=MULT),
            wait=[mm])
        t_b = b.op("tensor", lambda e, psB=psB: e.matmul(
            psB[:], C_onesF[0:1, :], SROW[0:1, 512:1024], start=True, stop=True),
            wait=[t_r])
        return b.op("scalar", lambda e, psB=psB: e.copy(BCAST[:], psB[:]),
                    wait=[t_b])

    def stats_rstd(src_bf, psM, psB, wait):
        mm = None
        for m in range(8):
            mm = b.op("tensor", lambda e, m=m, psM=psM: e.matmul(
                psM[0:1, :], C_onesB[:, 0:1], src_bf[:, 512 * m:512 * (m + 1)],
                start=(m == 0), stop=(m == 7)), wait=wait if m == 0 else None)
        t_srt = b.op("scalar", lambda e, psM=psM: e.activation(
            SROW[0:1, 512:1024], psM[0:1, :], AF.Sqrt, bias=C_eps[0:1, 0:1], scale=1.0 / D),
            wait=[mm])
        t_rc = b.op("vector", lambda e: e.reciprocal(
            SROW[0:1, 1024:1536], SROW[0:1, 512:1024]), wait=[t_srt])
        t_b = b.op("tensor", lambda e, psB=psB: e.matmul(
            psB[:], C_onesF[0:1, :], SROW[0:1, 1024:1536], start=True, stop=True),
            wait=[t_rc])
        return b.op("scalar", lambda e, psB=psB: e.copy(BCAST[:], psB[:]),
                    wait=[t_b])

    def layernorm_block(Xsrc, Xdst, SQbf, psM, psB, wait):
        t_cp = None
        for m in range(8):
            t_cp = b.op("scalar", lambda e, m=m: e.copy(
                SQbf[:, 512 * m:512 * (m + 1)], Xsrc[:, 512 * m:512 * (m + 1)]),
                wait=wait if m == 0 else None)
        t_mb = stats_mean(SQbf, psM, psB, [t_cp])
        t_c = None
        for m in range(8):
            t_c = b.op("vector", lambda e, m=m: e.tensor_tensor(
                FPC[:, 512 * m:512 * (m + 1)], Xsrc[:, 512 * m:512 * (m + 1)],
                BCAST[:], op=SUB), wait=[t_mb] if m == 0 else None)
        t_sq = None
        for m in range(8):
            t_sq = b.op("scalar", lambda e, m=m: e.activation(
                SQbf[:, 512 * m:512 * (m + 1)], FPC[:, 512 * m:512 * (m + 1)],
                AF.Square), wait=[t_c] if m == 0 else None)
        t_rb = stats_rstd(SQbf, psM, psB, [t_sq])
        t_o = None
        for m in range(8):
            t_o = b.op("vector", lambda e, m=m: e.tensor_tensor(
                Xdst[:, 512 * m:512 * (m + 1)], FPC[:, 512 * m:512 * (m + 1)],
                BCAST[:], op=MULT), wait=[t_rb] if m == 0 else None)
        return t_o

    g3 = {"t_h": [None, None], "t_x1": None, "x1dma": None, "hdma": None,
          "ln3dma": [None, None], "obmm": [None, None], "obdma": [None, None]}

    def load_ob(blk):
        par = blk % 2
        return b.ld_on("gOB", cpt(OB3[:, 4096 * par:4096 * par + 4096], 8),
                       ob_s[:, :, 512 * blk:512 * blk + 512]
                       .rearrange("h p t -> p h t"),
                       wait=[g3["obmm"][par]])

    ob_evs = [load_ob(0), load_ob(1)]
    for blk in range(8):
        par = blk % 2
        obo = 4096 * par
        FP = FPA if par == 0 else FPD
        t_ob = ob_evs[blk]
        prev_ev = None
        mm = None
        for m in range(8):
            for hh in range(8):
                wv = None
                if hh == 0:
                    wv = [prev_ev]
                    if m == 0:
                        wv.append(t_ob)
                mm = b.op("tensor", lambda e, m=m, hh=hh, obo=obo: e.matmul(
                    PS_A3[:], WO3[:, 1024 * hh + 128 * m:1024 * hh + 128 * (m + 1)],
                    OB3[:, obo + 512 * hh:obo + 512 * (hh + 1)],
                    start=(hh == 0), stop=(hh == 7)),
                    wait=wv)
            wv = [mm]
            if m == 0:
                wv += [g3["t_h"][par], g3["ln3dma"][par]]
            prev_ev = b.op("scalar", lambda e, m=m, FP=FP: e.copy(
                FP[:, 512 * m:512 * (m + 1)], PS_A3[:]), wait=wv)
        g3["obmm"][par] = mm
        if blk + 2 < 8:
            ob_evs.append(load_ob(blk + 2))
        t_ln2 = layernorm_block(FP, FP, BQ3, PS_M3, PS_B3,
                                [prev_ev, g3["hdma"]])
        t_xr = b.ld_on("dTB0", cpt(XRES[:], 8),
                       xT_s[:, :, 512 * blk:512 * blk + 512]
                       .rearrange("c p t -> p c t"), wait=[g3["t_x1"]])
        t_x1 = None
        for m in range(8):
            wv = None
            if m == 0:
                wv = [t_ln2, t_xr, g3["x1dma"]]
            t_x1 = b.op("vector", lambda e, m=m, FP=FP: e.tensor_tensor(
                FPB[:, 512 * m:512 * (m + 1)], FP[:, 512 * m:512 * (m + 1)],
                XRES[:, 512 * m:512 * (m + 1)], op=ADD), wait=wv)
        g3["t_x1"] = t_x1
        g3["x1dma"] = b.ld_on(
            "dTA0", x1_s[:, :, 512 * blk:512 * blk + 512]
            .rearrange("c p t -> p c t"), cpt(FPB[:], 8), wait=[t_x1])
        t_ln3 = layernorm_block(FPB, FP, BQ3, PS_M3, PS_B3, [t_x1])
        g3["ln3dma"][par] = b.ld_on(
            "dTA1", ln3_s[:, :, 512 * blk:512 * blk + 512]
            .rearrange("c p t -> p c t"), cpt(FP[:], 8), wait=[t_ln3])
        t_sq = None
        for m in range(8):
            wv = None
            if m == 0:
                wv = [t_ln3, g3["hdma"]]
            t_sq = b.op("scalar", lambda e, m=m, FP=FP: e.activation(
                BQ3[:, 512 * m:512 * (m + 1)], FP[:, 512 * m:512 * (m + 1)],
                AF.Square), wait=wv)
        t_rb = stats_rstd(BQ3, PS_M3, PS_B3, [t_sq])
        t_h = None
        for m in range(8):
            b.op("vector", lambda e, m=m, FP=FP: e.tensor_tensor(
                FPC[:, 512 * m:512 * (m + 1)], FP[:, 512 * m:512 * (m + 1)],
                BCAST[:], op=MULT), wait=[t_rb] if m == 0 else None)
            t_h = b.op("vector", lambda e, m=m: e.tensor_scalar(
                BQ3[:, 512 * m:512 * (m + 1)], FPC[:, 512 * m:512 * (m + 1)],
                C_g1[:, m:m + 1], None, op0=MULT))
        g3["t_h"][par] = t_h
        g3["hdma"] = b.ld_on(
            "dTB1", h_s[:, :, 512 * blk:512 * blk + 512]
            .rearrange("c p t -> p c t"), cpt(BQ3[:], 8), wait=[t_h])
    b.barrier()
    b.pop()
    b.pop()

    if upto < 4:
        return finish(nc, st, b)

    b.push()
    WS4 = b.sbuf("p4_ws", [128, 8192], BF16)
    TMPA4 = b.sbuf("p4_tmpa", [128, 1024], F32)
    TMPB4 = b.sbuf("p4_tmpb", [128, 1024], F32)
    b.push()
    CBUF = b.sbuf("p4_c", [128, 4 * T], BF16)
    QKI = b.sbuf("p4_qk", [128, 8 * T], BF16)
    VBF = b.sbuf("p4_v", [128, 4 * T], BF16)
    HB4 = b.sbuf("p4_hb", [128, 4096], BF16)
    PB4 = b.sbuf("p4_pbuf", [128, 4096], BF16)
    OST4 = b.sbuf("p4_ost", [128, 512], BF16)
    SQB4 = PB4

    b.push()
    PS_S4 = b.psum("p4_S", [128, 2048], F32)
    PS_O4 = b.psum("p4_O", [128, 512], F32)
    PS_L4 = b.psum("p4_l", [128, 512], F32)
    PS_B4 = b.psum("p4_B", [128, 512], F32)

    b.gld(WS4[:, 0:4096], din["wdownr"].rearrange("p d c f -> p (d c f)"))
    b.barrier()
    hb_guard = [None, None]
    prev_ev2 = [None, None]
    for tt in range(8):
        sl = tt % 2
        hoff = 0
        t_hb = b.ld_on(f"dTA{sl}",
                       cpt(HB4[:], 8),
                       h_s[:, :, 512 * tt:512 * tt + 512]
                       .rearrange("c p t -> p c t"), wait=[hb_guard[0], hb_guard[1]])
        for cc in range(4):
            par = cc % 2
            psb = [PS_O4, PS_L4][par]
            mm = None
            for dd in range(8):
                wv = None
                if dd == 0:
                    wv = [t_hb, prev_ev2[par]]
                mm = b.op("tensor", lambda e, cc=cc, dd=dd, hoff=hoff, psb=psb: e.matmul(
                    psb[:], WS4[:, 512 * dd + 128 * cc:512 * dd + 128 * (cc + 1)],
                    HB4[:, hoff + 512 * dd:hoff + 512 * (dd + 1)],
                    start=(dd == 0), stop=(dd == 7)), wait=wv)
            prev_ev2[par] = b.op("scalar", lambda e, cc=cc, tt=tt, psb=psb: e.copy(
                CBUF[:, T * cc + 512 * tt:T * cc + 512 * tt + 512], psb[:]),
                wait=[mm])
        hb_guard[sl] = mm
    b.barrier()
    if "c_s" in taps:
        for c in range(4):
            b.ld(c_s[c], CBUF[:, T * c:T * (c + 1)])
        b.barrier()

    b.gld(WS4[:, 0:2048], din["wqir"].rearrange("p c q f -> p (c q f)"))
    b.gld(WS4[:, 2048:4096], din["wkir"].rearrange("p c q f -> p (c q f)"))
    b.gld(WS4[:, 4096:6144], din["wvir"].rearrange("p c q f -> p (c q f)"))
    b.barrier()
    prev_ev2 = [None, None]
    idx = 0
    for (reg, wb) in ((0, 0), (4 * T, 2048)):
        for qc in range(4):
            for tt in range(8):
                par = idx % 2
                idx += 1
                psb = [PS_O4, PS_L4][par]
                mm = None
                for cc in range(4):
                    mm = b.op("tensor", lambda e, cc=cc, qc=qc, tt=tt, wb=wb,
                              psb=psb: e.matmul(
                        psb[:],
                        WS4[:, wb + 512 * cc + 128 * qc:wb + 512 * cc + 128 * (qc + 1)],
                        CBUF[:, T * cc + 512 * tt:T * cc + 512 * tt + 512],
                        start=(cc == 0), stop=(cc == 3)),
                        wait=[prev_ev2[par]] if cc == 0 else None)
                prev_ev2[par] = b.op("scalar", lambda e, reg=reg, qc=qc, tt=tt,
                                     psb=psb: e.copy(
                    QKI[:, reg + T * qc + 512 * tt:reg + T * qc + 512 * tt + 512],
                    psb[:]), wait=[mm])
    b.barrier()
    prev_ev2 = [None, None]
    for tc in range(32):
        par = tc % 2
        psb = [PS_O4, PS_L4][par]
        mm = None
        for cc in range(4):
            mm = b.op("tensor", lambda e, cc=cc, tc=tc, psb=psb: e.matmul(
                psb[:], CBUF[:, T * cc + 128 * tc:T * cc + 128 * (tc + 1)],
                WS4[:, 4096 + 512 * cc:4096 + 512 * (cc + 1)],
                start=(cc == 0), stop=(cc == 3)),
                wait=[prev_ev2[par]] if cc == 0 else None)
        prev_ev2[par] = b.op("scalar", lambda e, tc=tc, psb=psb: e.copy(
            VBF[:, 512 * tc:512 * (tc + 1)], psb[:]), wait=[mm])
    QSWI = HB4[:, 0:4096]
    fins4 = {}
    g4sw = {"swev": None, "bvlast": None}
    for reg in (0, 4 * T):
        for qc in range(4):
            sw_evs = [None] * 8
            swev = g4sw["swev"]
            for tt in range(8):
                wv = [swev]
                if tt == 0:
                    wv.append(g4sw["bvlast"])
                t_sw = b.op("tensor", lambda e, reg=reg, qc=qc, tt=tt: e.matmul(
                    PS_B4[:], C_swapI[:],
                    QKI[:, reg + T * qc + 512 * tt:reg + T * qc + 512 * tt + 512],
                    start=True, stop=True), wait=wv)
                swev = b.op("scalar", lambda e, tt=tt: e.copy(
                    QSWI[:, 512 * tt:512 * (tt + 1)], PS_B4[:]), wait=[t_sw])
                sw_evs[tt] = swev
            g4sw["swev"] = swev
            fins = rope_combine(QKI, reg + T * qc, QSWI, 0, din["ropeA_i"],
                                din["ropeB_i"], 8, sw_evs, TMPA4, TMPB4)
            fins4[(reg, qc)] = fins
            g4sw["bvlast"] = fins[7]

    last = None
    for g in range(4):
        tmaxs = []
        for (roff, sc, fk) in ((T * g, 8, (0, g)), ((4 + g) * T, 9, (4 * T, g))):
            for tt in range(8):
                t_sq = b.op("scalar", lambda e, roff=roff, tt=tt: e.activation(
                    SQB4[:, 0:512], QKI[:, roff + 512 * tt:roff + 512 * tt + 512],
                    AF.Square), wait=[last, fins4[fk][tt]])
                t_mm = b.op("tensor", lambda e: e.matmul(
                    PS_L4[0:1, :], C_onesB[:, 0:1], SQB4[:, 0:512], start=True,
                    stop=True), wait=[t_sq, last])
                t_mx = b.op("vector", lambda e: e.reduce_max(
                    STAT[0:1, 14:15], PS_L4[0:1, :], axis=AX.X), wait=[t_mm])
                if tt == 0:
                    last = b.op("vector", lambda e, sc=sc: e.tensor_copy(
                        STAT[0:1, sc:sc + 1], STAT[0:1, 14:15]))
                else:
                    last = b.op("vector", lambda e, sc=sc: e.tensor_tensor(
                        STAT[0:1, sc:sc + 1], STAT[0:1, sc:sc + 1],
                        STAT[0:1, 14:15], op=AluOpType.max))
            tmaxs.append(last)
        last = bias_from_norms(PS_S4, ISC_I, 4 + g, tmaxs)
    b.barrier()

    for g in range(4):
        p_guard = [None, None]
        ps_o_guard = None
        ps_l_guard = None
        prev_exp = None
        prev_odma = None
        for qt in range(8):
            nkc = 4 * qt + 4
            last_pv = None
            last_l = None
            tps = {}
            for kc in range(nkc + 1):
                if kc < nkc:
                    psl = kc % 2
                    mm = None
                    for hh in range(4):
                        mm = b.op("tensor", lambda e, g=g, hh=hh, kc=kc, qt=qt: e.matmul(
                            PS_S4[:, 512 * hh:512 * (hh + 1)],
                            QKI[32 * hh:32 * (hh + 1),
                                (4 + g) * T + 128 * kc:(4 + g) * T + 128 * (kc + 1)],
                            QKI[32 * hh:32 * (hh + 1),
                                T * g + 512 * qt:T * g + 512 * qt + 512],
                            start=True, stop=True, tile_position=(32 * hh, 0)),
                            wait=[prev_exp] if hh == 0 else None)
                    t_exp = b.op("scalar", lambda e, g=g, psl=psl: e.activation(
                        PB4[:, 2048 * psl:2048 * (psl + 1)], PS_S4[:], AF.Exp,
                        bias=CBIAS[:, 4 + g:5 + g], scale=float(ISC_I)),
                        wait=[mm, p_guard[psl]])
                    prev_exp = t_exp
                    t_p = t_exp
                    j = kc - 4 * qt
                    if j >= 0:
                        for hh4 in range(4):
                            t_p = b.op("vector", lambda e, psl=psl, j=j,
                                       hh4=hh4: e.tensor_tensor(
                                PB4[:, 2048 * psl + 512 * hh4:2048 * psl + 512 * (hh4 + 1)],
                                PB4[:, 2048 * psl + 512 * hh4:2048 * psl + 512 * (hh4 + 1)],
                                C_masks[:, 512 * j:512 * (j + 1)], op=MULT),
                                wait=[t_exp] if hh4 == 0 else None)
                    tps[kc] = t_p
                pk = kc - 1
                if pk >= 0:
                    psl = pk % 2
                    for hh in range(4):
                        wv2 = [tps[pk]] if hh == 0 else None
                        if pk == 0 and hh == 0:
                            wv2 = [tps[pk], ps_o_guard]
                        last_pv = b.op("tensor", lambda e, g=g, hh=hh, pk=pk, psl=psl,
                                       nkc=nkc: e.matmul(
                            PS_O4[32 * hh:32 * (hh + 1), :],
                            VBF[:, 512 * pk + 128 * g + 32 * hh:
                                512 * pk + 128 * g + 32 * (hh + 1)],
                            PB4[:, 2048 * psl + 512 * hh:2048 * psl + 512 * (hh + 1)],
                            start=(pk == 0), stop=(pk == nkc - 1),
                            tile_position=(0, 32 * hh),
                            skip_group_check=True), wait=wv2)
                    for hh in range(4):
                        wv2 = None
                        if pk == 0 and hh == 0:
                            wv2 = [ps_l_guard]
                        last_l = b.op("tensor", lambda e, hh=hh, pk=pk, psl=psl,
                                      nkc=nkc: e.matmul(
                            PS_L4[32 * hh:32 * hh + 1, :], C_onesB[:, 0:1],
                            PB4[:, 2048 * psl + 512 * hh:2048 * psl + 512 * (hh + 1)],
                            start=(pk == 0), stop=(pk == nkc - 1),
                            tile_position=(0, 32 * hh),
                            skip_group_check=True), wait=wv2)
                    p_guard[psl] = last_l
            t_rec = None
            for hh in range(4):
                t_rec = b.op("vector", lambda e, hh=hh: e.reciprocal(
                    RECROW[32 * hh:32 * hh + 1, :], PS_L4[32 * hh:32 * hh + 1, :]),
                    wait=[last_l] if hh == 0 else None)
            ps_l_guard = t_rec
            t_bc = None
            for hh in range(4):
                t_bc = b.op("tensor", lambda e, hh=hh: e.matmul(
                    PS_B4[32 * hh:32 * (hh + 1), :],
                    C_onesF[32 * hh:32 * hh + 1, 0:32],
                    RECROW[32 * hh:32 * hh + 1, :],
                    start=True, stop=True, tile_position=(32 * hh, 32 * hh)),
                    wait=[t_rec] if hh == 0 else None)
            t_bcev = b.op("scalar", lambda e: e.copy(BCAST[:], PS_B4[:]),
                          wait=[t_bc])
            ps_o_guard = b.op("vector", lambda e: e.tensor_tensor(
                OST4[:], PS_O4[:], BCAST[:], op=MULT),
                wait=[t_bcev, last_pv, prev_odma])
            prev_odma = b.ld(o2_s[g][:, 512 * qt:512 * qt + 512], OST4[:],
                             wait=[ps_o_guard])
        b.barrier()
    b.pop()
    b.pop()

    b.gld(WS4[:, 0:4096], din["wupr"].rearrange("p l m f -> p (l m f)"))
    b.gld(WS4[:, 6144:6208], din["routerr"].rearrange("p c e -> p (c e)"))
    b.barrier()
    b.push()
    OSTR = b.sbuf("p4_ostr", [128, 2048], BF16)
    FPA4 = b.sbuf("p4_fpa", [128, 4096], F32)
    FPB4 = b.sbuf("p4_fpb", [128, 4096], F32)
    FPC4 = b.sbuf("p4_fpc", [128, 4096], F32)
    XR4 = b.sbuf("p4_xr", [128, 4096], F32)
    BQ4 = b.sbuf("p4_bq", [128, 4096], BF16)
    TB4 = b.sbuf("p4_tb", [128, 8192], BF16)
    U4 = b.sbuf("p4_u", [128, 16], F32)
    G4 = b.sbuf("p4_g", [128, 16], BF16)
    M4 = b.sbuf("p4_m", [128, 48], F32)
    PS_M4 = b.psum("p4_m2", [128, 512], F32)
    PS_U4 = b.psum("p4_u2", [128, 1024], F32)
    PS_R4 = b.psum("p4_r", [128, 1024], F32)
    PS_G4 = b.psum("p4_gt", [128, 2048], BF16)

    def stats_rstd4(src_bf, wait):
        mm = None
        for m in range(8):
            mm = b.op("tensor", lambda e, m=m: e.matmul(
                PS_M4[0:1, :], C_onesB[:, 0:1], src_bf[:, 512 * m:512 * (m + 1)],
                start=(m == 0), stop=(m == 7)), wait=wait if m == 0 else None)
        t_srt = b.op("scalar", lambda e: e.activation(
            SROW[0:1, 512:1024], PS_M4[0:1, :], AF.Sqrt, bias=C_eps[0:1, 0:1], scale=1.0 / D),
            wait=[mm])
        t_rc = b.op("vector", lambda e: e.reciprocal(
            SROW[0:1, 1024:1536], SROW[0:1, 512:1024]), wait=[t_srt])
        t_b = b.op("tensor", lambda e: e.matmul(
            PS_M4[:], C_onesF[0:1, :], SROW[0:1, 1024:1536], start=True, stop=True),
            wait=[t_rc])
        return b.op("scalar", lambda e: e.copy(BCAST[:], PS_M4[:]), wait=[t_b])

    rg = {"tu": [None, None], "tg": [None, None], "tgc": [None, None]}

    def emit_router(blk, t_t_blk):
        tbo = 4096 * (blk % 2)
        for tc in range(4):
            sl = tc % 2
            uo, mo, ro, go, gpo = 8 * sl, 24 * sl, 512 * sl, 8 * sl, 1024 * sl
            U = U4[:, uo:uo + 8]
            mm = None
            for dd in range(8):
                wv = None
                if dd == 0:
                    wv = [rg["tu"][sl]]
                    if tc == 0:
                        wv.append(t_t_blk)
                mm = b.op("tensor", lambda e, dd=dd, tc=tc, tbo=tbo, ro=ro: e.matmul(
                    PS_R4[:, ro:ro + 8],
                    TB4[:, tbo + 512 * dd + 128 * tc:tbo + 512 * dd + 128 * (tc + 1)],
                    WS4[:, 6144 + 8 * dd:6144 + 8 * (dd + 1)],
                    start=(dd == 0), stop=(dd == 7)), wait=wv)
            t_u = b.op("scalar", lambda e, U=U, ro=ro: e.copy(U, PS_R4[:, ro:ro + 8]),
                       wait=[mm, rg["tg"][sl]])
            rg["tu"][sl] = t_u
            t_m1 = b.op("vector", lambda e, U=U, mo=mo: e.reduce_max(
                M4[:, mo:mo + 1], U, axis=AX.X), wait=[t_u])
            t_ms1 = b.op("vector", lambda e, mo=mo: e.tensor_scalar(
                M4[:, mo + 1:mo + 2], M4[:, mo:mo + 1], -1.0, None, op0=MULT),
                wait=[t_m1])
            t_k1 = b.op("vector", lambda e, U=U, mo=mo: e.tensor_scalar(
                M4[:, mo + 8:mo + 16], U, M4[:, mo:mo + 1], 1.0,
                op0=AluOpType.is_ge, op1=AluOpType.min), wait=[t_m1])
            t_u2 = b.op("vector", lambda e, U=U, mo=mo: e.scalar_tensor_tensor(
                M4[:, mo + 16:mo + 24], M4[:, mo + 8:mo + 16], -1e30, U,
                op0=MULT, op1=ADD), wait=[t_k1])
            t_m2 = b.op("vector", lambda e, mo=mo: e.reduce_max(
                M4[:, mo + 2:mo + 3], M4[:, mo + 16:mo + 24], axis=AX.X),
                wait=[t_u2])
            t_e = b.op("scalar", lambda e, U=U, mo=mo: e.activation(
                U, U, AF.Exp, bias=M4[:, mo + 1:mo + 2]), wait=[t_m2, t_ms1])
            t_ed = b.op("scalar", lambda e, mo=mo: e.activation(
                M4[:, mo + 3:mo + 4], M4[:, mo + 2:mo + 3], AF.Exp,
                bias=M4[:, mo + 1:mo + 2]))
            t_ed1 = b.op("vector", lambda e, mo=mo: e.tensor_scalar(
                M4[:, mo + 4:mo + 5], M4[:, mo + 3:mo + 4], 1.0, None, op0=ADD),
                wait=[t_ed])
            t_rd = b.op("vector", lambda e, mo=mo: e.reciprocal(
                M4[:, mo + 5:mo + 6], M4[:, mo + 4:mo + 5]), wait=[t_ed1])
            t_k2 = b.op("vector", lambda e, U=U, mo=mo: e.tensor_scalar(
                M4[:, mo + 8:mo + 16], U, M4[:, mo + 3:mo + 4], 1.0,
                op0=AluOpType.is_ge, op1=AluOpType.min), wait=[t_e, t_ed])
            b.op("vector", lambda e, U=U, mo=mo: e.tensor_tensor(
                U, U, M4[:, mo + 8:mo + 16], op=MULT), wait=[t_k2])
            t_g = b.op("vector", lambda e, U=U, mo=mo, go=go: e.tensor_scalar(
                G4[:, go:go + 8], U, M4[:, mo + 5:mo + 6], None, op0=MULT),
                wait=[t_rd])
            rg["tg"][sl] = t_g
            t_tr = b.op("tensor", lambda e, go=go, gpo=gpo: e.transpose(
                PS_G4[0:8, gpo:gpo + 128], G4[:, go:go + 8], C_identB[:]),
                wait=[t_g, rg["tgc"][sl]])
            rg["tgc"][sl] = b.op("scalar", lambda e, blk=blk, tc=tc, gpo=gpo: e.copy(
                GATE[:, 512 * blk + 128 * tc:512 * blk + 128 * (tc + 1)],
                PS_G4[0:8, gpo:gpo + 128]), wait=[t_tr])

    g4g = {"t_xf": None, "lastwup": None, "xfdma": None, "tsdma": [None, None],
           "t_cp": None, "statmm": None, "t_t": [None, None], "umm": [None, None]}
    for blk in range(8):
        tbo = 4096 * (blk % 2)
        t_xr = b.ld_on("dTB0", cpt(XR4[:], 8),
                       ln3_s[:, :, 512 * blk:512 * blk + 512]
                       .rearrange("c p t -> p c t"), wait=[g4g["t_xf"]])
        t_os = b.ld_on("dTB1", cpt(OSTR[:], 4),
                       o2_s[:, :, 512 * blk:512 * blk + 512]
                       .rearrange("g p t -> p g t"), wait=[g4g["lastwup"]])
        prev_ev2 = [None, None]
        mm = None
        for m in range(8):
            par = m % 2
            mm = None
            for lc in range(4):
                wv = None
                if lc == 0:
                    wv = [prev_ev2[par], t_os]
                mm = b.op("tensor", lambda e, m=m, lc=lc, par=par: e.matmul(
                    PS_U4[:, 512 * par:512 * (par + 1)],
                    WS4[:, 1024 * lc + 128 * m:1024 * lc + 128 * (m + 1)],
                    OSTR[:, 512 * lc:512 * (lc + 1)],
                    start=(lc == 0), stop=(lc == 3)), wait=wv)
            wv = [mm]
            if m == 0:
                wv.append(g4g["t_xf"])
            prev_ev2[par] = b.op("scalar", lambda e, m=m, par=par: e.copy(
                FPA4[:, 512 * m:512 * (m + 1)],
                PS_U4[:, 512 * par:512 * (par + 1)]), wait=wv)
        g4g["lastwup"] = mm
        t_xf = None
        for m in range(8):
            wv = None
            if m == 0:
                wv = [prev_ev2[0], prev_ev2[1], t_xr, g4g["xfdma"],
                      g4g["t_cp"]]
            t_xf = b.op("vector", lambda e, m=m: e.tensor_tensor(
                FPB4[:, 512 * m:512 * (m + 1)], FPA4[:, 512 * m:512 * (m + 1)],
                XR4[:, 512 * m:512 * (m + 1)], op=ADD), wait=wv)
        g4g["t_xf"] = t_xf
        g4g["xfdma"] = b.ld_on(
            "dTA0", xf_s[:, :, 512 * blk:512 * blk + 512]
            .rearrange("c p t -> p c t"), cpt(FPB4[:], 8), wait=[t_xf])
        t_cp = None
        for m in range(8):
            wv = None
            if m == 0:
                wv = [t_xf, g4g["statmm"]]
            t_cp = b.op("scalar", lambda e, m=m: e.copy(
                BQ4[:, 512 * m:512 * (m + 1)], FPB4[:, 512 * m:512 * (m + 1)]),
                wait=wv)
        g4g["t_cp"] = t_cp
        t_sq2 = None
        for m in range(8):
            t_sq2 = b.op("scalar", lambda e, m=m: e.activation(
                BQ4[:, 512 * m:512 * (m + 1)], BQ4[:, 512 * m:512 * (m + 1)],
                AF.Square), wait=[t_cp] if m == 0 else None)
        t_rb = stats_rstd4(BQ4, [t_sq2])
        g4g["statmm"] = t_rb
        t_t = None
        sl = blk % 2
        for m in range(8):
            wv = None
            if m == 0:
                wv = [t_rb, g4g["tsdma"][sl], g4g["umm"][sl]]
            b.op("vector", lambda e, m=m: e.tensor_tensor(
                FPC4[:, 512 * m:512 * (m + 1)], FPB4[:, 512 * m:512 * (m + 1)],
                BCAST[:], op=MULT), wait=wv)
            t_t = b.op("vector", lambda e, m=m, tbo=tbo: e.tensor_scalar(
                TB4[:, tbo + 512 * m:tbo + 512 * (m + 1)],
                FPC4[:, 512 * m:512 * (m + 1)],
                C_g2[:, m:m + 1], None, op0=MULT))
        g4g["t_t"][sl] = t_t
        g4g["tsdma"][sl] = b.ld_on(
            "dTA1", t_s[:, :, 512 * blk:512 * blk + 512]
            .rearrange("c p t -> p c t"),
            cpt(TB4[:, tbo:tbo + 4096], 8), wait=[t_t])
        if blk >= 1:
            emit_router(blk - 1, g4g["t_t"][(blk - 1) % 2])
            g4g["umm"][(blk - 1) % 2] = rg["tu"][1]
    emit_router(7, g4g["t_t"][1])
    b.barrier()
    b.pop()
    b.pop()
    b.pop()
    if "gate_s" in taps:
        b.ld(gate_s[:, :], GATE[:, :])
        b.barrier()

    if upto < 5:
        return finish(nc, st, b)

    b.push()
    Y5 = b.sbuf("p5_y", [128, 8192], F32)
    TS5 = b.sbuf("p5_t", [128, 8192], BF16)
    W15 = b.sbuf("p5_w1", [128, 16384], BF16)
    W35 = b.sbuf("p5_w3", [128, 16384], BF16)
    W25 = b.sbuf("p5_w2", [128, 16384], BF16)

    def issue_w13_chunk(ee, fc4, wait):
        b.gld_on("gW13", W15[:, 4096 * fc4:4096 * (fc4 + 1)]
                 .rearrange("p (fc x) -> p fc x", fc=4),
                 din["w1r"][ee, 4 * fc4:4 * (fc4 + 1)]
                 .rearrange("fc p d f -> p fc (d f)"),
                 wait=wait)
        return b.gld_on("gW13", W35[:, 4096 * fc4:4096 * (fc4 + 1)]
                        .rearrange("p (fc x) -> p fc x", fc=4),
                        din["w3r"][ee, 4 * fc4:4 * (fc4 + 1)]
                        .rearrange("fc p d f -> p fc (d f)"))

    def issue_w2(ee, wait):
        t = None
        for fc4 in range(4):
            t = b.gld_on("gW2", W25[:, 4096 * fc4:4096 * (fc4 + 1)]
                         .rearrange("p (fc m) -> p fc m", fc=4),
                         din["w2r"][ee][:, 4 * fc4:4 * (fc4 + 1), :],
                         wait=wait if fc4 == 0 else None)
        return t

    pf = {"w13ct": None, "w2ct": None, "lastA": None}

    def emit_super(s):
        t_ts = b.ld_on("gTS", cpt(TS5[:], 8),
                       t_s[:, :, 1024 * s:1024 * (s + 1)]
                       .rearrange("c p t -> p c t"), wait=[pf["lastA"]])
        b.barrier(skip_gw=True)
        b.push()
        HID5 = b.sbuf(f"p5_hid{s}", [128, 8192], BF16)
        GB5 = b.sbuf(f"p5_gb{s}", [128, 1024], BF16)
        SIL5 = b.sbuf(f"p5_sil{s}", [128, 1024], F32)
        TMP5 = b.sbuf(f"p5_tmp{s}", [128, 1024], F32)
        PS_A1 = b.psum(f"p5_a1{s}", [128, 1024], F32)
        PS_A3p = b.psum(f"p5_a3{s}", [128, 1024], F32)
        PS_G5 = b.psum(f"p5_g{s}", [128, 512], F32)
        PS_Y5 = b.psum(f"p5_yp{s}", [128, 1024], F32)

        b.op("vector", lambda e: e.memset(Y5[:], 0.0))

        if pf["w13ct"] is None:
            w13_ct = [issue_w13_chunk(0, fc4, None) for fc4 in range(4)]
            w2_tgt = issue_w2(0, None)
        else:
            w13_ct, w2_tgt = pf["w13ct"], pf["w2ct"]
            pf["w13ct"] = pf["w2ct"] = None
        sil_g = [None, None]
        tmp_g = [None, None]
        y_g = [None, None]
        hid_done = [None, None]
        B_done = [None, None]
        prev_hid_final = None
        t_gbe = None
        lastA = None
        lastB = None
        for ee in range(8):
            for tb in range(2):
                t_gb = b.op("tensor", lambda e, ee=ee, s=s, tb=tb: e.matmul(
                    PS_G5[:], C_epat[:, 128 * ee:128 * (ee + 1)],
                    GATE[:, 1024 * s + 512 * tb:1024 * s + 512 * tb + 512],
                    start=True, stop=True), wait=[t_gbe])
                t_gbe = b.op("scalar", lambda e, tb=tb: e.copy(
                    GB5[:, 512 * tb:512 * (tb + 1)], PS_G5[:]),
                    wait=[t_gb, prev_hid_final])
            nxt_w13 = []
            has_next = (ee < 7) or (s < 3)
            nee = (ee + 1) % 8
            for tb in range(2):
                for fc in range(16):
                    par = fc % 2
                    mm1 = None
                    for dd in range(8):
                        wv = None
                        if dd == 0:
                            wv = [sil_g[par]]
                            if tb == 0 and fc % 4 == 0:
                                wv.append((b.sem["gW13"], w13_ct[fc // 4][1]))
                            if tb == 0 and fc == 0:
                                wv.append(t_ts)
                        mm1 = b.op("tensor", lambda e, fc=fc, dd=dd, tb=tb,
                                   par=par: e.matmul(
                            PS_A1[:, 512 * par:512 * (par + 1)],
                            W15[:, 1024 * fc + 128 * dd:1024 * fc + 128 * (dd + 1)],
                            TS5[:, 1024 * dd + 512 * tb:1024 * dd + 512 * tb + 512],
                            start=(dd == 0), stop=(dd == 7)), wait=wv)
                    mm3 = None
                    for dd in range(8):
                        mm3 = b.op("tensor", lambda e, fc=fc, dd=dd, tb=tb,
                                   par=par: e.matmul(
                            PS_A3p[:, 512 * par:512 * (par + 1)],
                            W35[:, 1024 * fc + 128 * dd:1024 * fc + 128 * (dd + 1)],
                            TS5[:, 1024 * dd + 512 * tb:1024 * dd + 512 * tb + 512],
                            start=(dd == 0), stop=(dd == 7)),
                            wait=[tmp_g[par]] if dd == 0 else None)
                    lastA = mm3
                    if tb == 1 and fc % 4 == 3 and has_next:
                        nxt_w13.append(issue_w13_chunk(nee, fc // 4, [mm3]))
                    t_sil = b.op("scalar", lambda e, par=par: e.activation(
                        SIL5[:, 512 * par:512 * (par + 1)],
                        PS_A1[:, 512 * par:512 * (par + 1)], AF.Silu),
                        wait=[mm1])
                    wv = [mm3]
                    if tb == 0 and fc == 0:
                        wv.append(t_gbe)
                    t_tmp = b.op("vector", lambda e, tb=tb, par=par: e.tensor_tensor(
                        TMP5[:, 512 * par:512 * (par + 1)],
                        PS_A3p[:, 512 * par:512 * (par + 1)],
                        GB5[:, 512 * tb:512 * (tb + 1)], op=MULT), wait=wv)
                    wv = [t_sil]
                    if fc == 0:
                        wv.append(B_done[tb])
                    t_hid = b.op("vector", lambda e, fc=fc, par=par:
                                 e.tensor_tensor(
                        HID5[:, 512 * fc:512 * (fc + 1)],
                        TMP5[:, 512 * par:512 * (par + 1)],
                        SIL5[:, 512 * par:512 * (par + 1)], op=MULT), wait=wv)
                    sil_g[par] = t_hid
                    tmp_g[par] = t_hid
                hid_done[tb] = t_hid
                prev_hid_final = t_hid
                lastBm = None
                for m in range(8):
                    ypar = m % 2
                    mm = None
                    for fc in range(16):
                        wv = None
                        if fc == 0:
                            wv = [hid_done[tb], y_g[ypar]]
                            if tb == 0 and m == 0:
                                wv.append((b.sem["gW2"], w2_tgt[1]))
                        mm = b.op("tensor", lambda e, m=m, fc=fc, ypar=ypar:
                                  e.matmul(
                            PS_Y5[:, 512 * ypar:512 * (ypar + 1)],
                            W25[:, 1024 * fc + 128 * m:1024 * fc + 128 * (m + 1)],
                            HID5[:, 512 * fc:512 * (fc + 1)],
                            start=(fc == 0), stop=(fc == 15)), wait=wv)
                    y_g[ypar] = b.op("vector", lambda e, m=m, tb=tb, ypar=ypar:
                                     e.tensor_tensor(
                        Y5[:, 1024 * m + 512 * tb:1024 * m + 512 * tb + 512],
                        Y5[:, 1024 * m + 512 * tb:1024 * m + 512 * tb + 512],
                        PS_Y5[:, 512 * ypar:512 * (ypar + 1)], op=ADD),
                        wait=[mm])
                    lastBm = mm
                B_done[tb] = lastBm
                lastB = lastBm
            if has_next:
                w13_ct = nxt_w13
                w2_tgt = issue_w2(nee, [lastB])
        if s < 3:
            pf["w13ct"], pf["w2ct"] = w13_ct, w2_tgt
        pf["lastA"] = lastA
        b.barrier(skip_gw=True)

        if "y_s" in taps:
            for c in range(8):
                b.ld(y_s[c, :, 1024 * s:1024 * (s + 1)],
                     Y5[:, 1024 * c:1024 * (c + 1)])
            b.barrier()

        b.pop()
        b.push()
        X16 = b.sbuf(f"p6_x1{s}", [128, 2048], F32)
        OF6 = b.sbuf(f"p6_of{s}", [128, 4096], F32)
        BQ6 = b.sbuf(f"p6_bq{s}", [128, 4096], BF16)
        OS6 = b.sbuf(f"p6_os{s}", [128, 2048], F32)
        PS_M6 = b.psum(f"p6_m{s}", [128, 512], F32)
        PS_B6 = b.psum(f"p6_b{s}", [128, 512], F32)
        PS_T6 = b.psum(f"p6_t{s}", [128, 1024], F32)

        def stats6(src_bf, kind, wait):
            mm = None
            for m in range(8):
                mm = b.op("tensor", lambda e, m=m: e.matmul(
                    PS_M6[0:1, :], C_onesB[:, 0:1],
                    src_bf[:, 512 * m:512 * (m + 1)],
                    start=(m == 0), stop=(m == 7)), wait=wait if m == 0 else None)
            if kind == "mean":
                t_r = b.op("vector", lambda e: e.tensor_scalar(
                    SROW[0:1, 512:1024], PS_M6[0:1, :], 1.0 / D, None, op0=MULT),
                    wait=[mm])
                t_b = b.op("tensor", lambda e: e.matmul(
                    PS_B6[:], C_onesF[0:1, :], SROW[0:1, 512:1024],
                    start=True, stop=True), wait=[t_r])
            else:
                t_srt = b.op("scalar", lambda e: e.activation(
                    SROW[0:1, 512:1024], PS_M6[0:1, :], AF.Sqrt,
                    bias=C_eps[0:1, 0:1], scale=1.0 / D), wait=[mm])
                t_rc = b.op("vector", lambda e: e.reciprocal(
                    SROW[0:1, 1024:1536], SROW[0:1, 512:1024]), wait=[t_srt])
                t_b = b.op("tensor", lambda e: e.matmul(
                    PS_B6[:], C_onesF[0:1, :], SROW[0:1, 1024:1536],
                    start=True, stop=True), wait=[t_rc])
            return b.op("scalar", lambda e: e.copy(BCAST[:], PS_B6[:]),
                        wait=[t_b])

        ep = {"osdma": [None, None], "tp": None, "x2": None, "x2h": None}
        for tb in range(2):
            blk = 2 * s + tb
            t_xf = b.ld_on("dTA1", cpt(OF6[:], 8),
                           xf_s[:, :, 512 * blk:512 * blk + 512]
                           .rearrange("c p t -> p c t"),
                           wait=[ep["tp"]])
            t_x1a = b.ld_on(
                "dTA0", cpt(X16[:], 4),
                x1_s[0:4, :, 512 * blk:512 * blk + 512]
                .rearrange("c p t -> p c t"), wait=[ep["x2h"]])
            t_of = None
            for m in range(8):
                t_of = b.op("vector", lambda e, m=m, tb=tb: e.tensor_tensor(
                    OF6[:, 512 * m:512 * (m + 1)], OF6[:, 512 * m:512 * (m + 1)],
                    Y5[:, 1024 * m + 512 * tb:1024 * m + 512 * tb + 512], op=ADD),
                    wait=[t_xf] if m == 0 else None)
            t_cp = None
            for m in range(8):
                t_cp = b.op("scalar", lambda e, m=m: e.copy(
                    BQ6[:, 512 * m:512 * (m + 1)], OF6[:, 512 * m:512 * (m + 1)]),
                    wait=[t_of] if m == 0 else None)
            t_mb = stats6(BQ6, "mean", [t_cp])
            t_c = None
            for m in range(8):
                t_c = b.op("vector", lambda e, m=m: e.tensor_tensor(
                    OF6[:, 512 * m:512 * (m + 1)], OF6[:, 512 * m:512 * (m + 1)],
                    BCAST[:], op=SUB), wait=[t_mb] if m == 0 else None)
            t_sq = None
            for m in range(8):
                t_sq = b.op("scalar", lambda e, m=m: e.activation(
                    BQ6[:, 512 * m:512 * (m + 1)], OF6[:, 512 * m:512 * (m + 1)],
                    AF.Square), wait=[t_c] if m == 0 else None)
            t_rb = stats6(BQ6, "rstd", [t_sq])
            t_x2 = None
            t_x1b = None
            for m in range(8):
                b.op("vector", lambda e, m=m: e.tensor_tensor(
                    OF6[:, 512 * m:512 * (m + 1)], OF6[:, 512 * m:512 * (m + 1)],
                    BCAST[:], op=MULT), wait=[t_rb] if m == 0 else None)
                wv = None
                if m == 0:
                    wv = [t_x1a]
                elif m == 4:
                    wv = [t_x1b]
                t_x2 = b.op("vector", lambda e, m=m: e.tensor_tensor(
                    OF6[:, 512 * m:512 * (m + 1)], OF6[:, 512 * m:512 * (m + 1)],
                    X16[:, 512 * (m % 4):512 * (m % 4 + 1)], op=ADD), wait=wv)
                if m == 3:
                    t_x1b = b.ld_on(
                        "dTA0", cpt(X16[:], 4),
                        x1_s[4:8, :, 512 * blk:512 * blk + 512]
                        .rearrange("c p t -> p c t"), wait=[t_x2])
            ep["x2h"] = t_x2
            tp = None
            for tc in range(4):
                osl = tc % 2
                tpl = None
                for c in range(8):
                    wv = None
                    if c == 0:
                        wv = [t_x2] if tc == 0 else [ep["tp"]]
                        wv = [t_x2, ep["tp"]] if tc == 0 else [ep["tp"]]
                    tpl = b.op("tensor", lambda e, c=c, tc=tc: e.transpose(
                        PS_T6[:, 128 * c:128 * (c + 1)],
                        OF6[:, 512 * c + 128 * tc:512 * c + 128 * (tc + 1)],
                        C_identF[:]), wait=wv)
                t_ev = b.op("scalar", lambda e, osl=osl: e.copy(
                    OS6[:, 1024 * osl:1024 * (osl + 1)], PS_T6[:]),
                    wait=[tpl, ep["osdma"][osl]])
                ep["tp"] = t_ev
                ep["osdma"][osl] = b.ld_on(
                    f"dTB{osl}",
                    outh[512 * blk + 128 * tc:512 * blk + 128 * (tc + 1), :],
                    OS6[:, 1024 * osl:1024 * (osl + 1)], wait=[t_ev])
        b.barrier(skip_gw=True)
        b.pop()

    for s in range(4):
        emit_super(s)
    b.pop()
    return finish(nc, st, b)


def finish(nc, st, b):
    b.barrier()
    with nc.Block() as block:
        @block.tensor
        def _(e):
            for f in b.ops["tensor"]:
                f(e)

        @block.scalar
        def _(e):
            for f in b.ops["scalar"]:
                f(e)

        @block.vector
        def _(e):
            for f in b.ops["vector"]:
                f(e)

        @block.gpsimd
        def _(e):
            for f in b.ops["gpsimd"]:
                f(e)

        @block.sync
        def _(e):
            for f in b.ops["sync"]:
                f(e)

    while b.stacks:
        b.pop()
    st.close()
    return nc


def kernel(**inputs):
    x = np.ascontiguousarray(inputs["L0_attn_x"])
    consts = host_constants()
    wr = host_weights({k: np.ascontiguousarray(v) for k, v in inputs.items()})
    nc = build_program(taps=tuple(DEBUG_TAPS))
    in_maps = []
    for bch in range(8):
        m = {"x": np.ascontiguousarray(x[bch])}
        m.update(consts)
        m.update(wr)
        in_maps.append(m)
    res = run_bass_kernel_spmd(nc, in_maps, core_ids=list(range(8)))
    out = np.stack([np.asarray(res.results[i]["out"]) for i in range(8)], axis=0)
    return out.astype(np.float32)


# version 24
# speedup vs baseline: 1.0268x; 1.0268x over previous
import numpy as np
import ml_dtypes
import concourse.bass as bass
import concourse.mybir as mybir
from concourse.bass_utils import run_bass_kernel_spmd
from concourse.alu_op_type import AluOpType
from contextlib import ExitStack

F32 = mybir.dt.float32
BF16 = mybir.dt.bfloat16
AF = mybir.ActivationFunctionType
AX = mybir.AxisListType
BF = ml_dtypes.bfloat16
MULT = AluOpType.mult
ADD = AluOpType.add
SUB = AluOpType.subtract

T = 4096
D = 1024
EPS = 1e-5
ISC_O = 1.0 / 128.0
ISC_I = float(1.0 / np.sqrt(32.0))

DEBUG_TAPS = []
DEBUG_HEAD = 0

ENGINES = ("tensor", "scalar", "vector", "gpsimd", "sync")

PERM32 = np.concatenate([np.arange(0, 32, 2), np.arange(1, 32, 2)])


def host_constants():
    c = {}
    pos = np.arange(T, dtype=np.float64)[None, :]
    f_o = np.power(10000.0, -np.arange(64, dtype=np.float64) / 64.0)[:, None]
    ang = f_o * pos
    c["ropeA_o"] = np.concatenate([np.cos(ang), np.cos(ang)], 0).astype(np.float32)
    c["ropeB_o"] = np.concatenate([-np.sin(ang), np.sin(ang)], 0).astype(np.float32)
    inv = 1.0 / 10000.0 ** (np.arange(0, 32, 2, dtype=np.float64) / 32.0)
    ang = inv[:, None] * pos
    A1 = np.concatenate([np.cos(ang), np.cos(ang)], 0)
    B1 = np.concatenate([-np.sin(ang), np.sin(ang)], 0)
    c["ropeA_i"] = np.tile(A1, (4, 1)).astype(np.float32)
    c["ropeB_i"] = np.tile(B1, (4, 1)).astype(np.float32)
    p = np.arange(128)[:, None]
    col = np.arange(512)[None, :]
    masks = np.stack([(p <= col - 128 * j).astype(np.float32) for j in range(4)])
    c["masks"] = masks.astype(BF)
    sw = np.zeros((128, 128), np.float32)
    for i in range(64):
        sw[i + 64, i] = 1.0
        sw[i, i + 64] = 1.0
    c["swapO"] = sw.astype(BF)
    sw = np.zeros((128, 128), np.float32)
    for s in range(0, 128, 32):
        for i in range(16):
            sw[s + 16 + i, s + i] = 1.0
            sw[s + i, s + 16 + i] = 1.0
    c["swapI"] = sw.astype(BF)
    c["identF"] = np.eye(128, dtype=np.float32)
    c["identB"] = np.eye(128, dtype=np.float32).astype(BF)
    c["onesB"] = np.ones((128, 128), np.float32).astype(BF)
    c["onesF"] = np.ones((128, 128), np.float32)
    ep = np.zeros((8, 1024), np.float32)
    for e in range(8):
        ep[e, e * 128:(e + 1) * 128] = 1.0
    c["epat"] = ep.astype(BF)
    c["epsc"] = np.full((128, 1), 1e-5, np.float32)
    return c


def host_weights(w):
    r = {}
    r["wqr"] = np.ascontiguousarray(w["L0_attn_wq"].reshape(8, 8, 128, 128).transpose(0, 2, 1, 3))
    r["wkr"] = np.ascontiguousarray(w["L0_attn_wk"].reshape(8, 8, 128, 128).transpose(0, 2, 1, 3))
    r["wvr"] = np.ascontiguousarray(w["L0_attn_wv"].reshape(8, 8, 128, 128).transpose(0, 2, 1, 3))
    r["wor"] = np.ascontiguousarray(w["L0_attn_wo"])
    r["wdownr"] = np.ascontiguousarray(w["L0_ffn_w_down"].reshape(8, 128, 4, 128).transpose(1, 0, 2, 3))
    wq_p = w["L0_ffn_wq"].reshape(512, 16, 32)[:, :, PERM32].reshape(512, 512)
    wk_p = w["L0_ffn_wk"].reshape(512, 16, 32)[:, :, PERM32].reshape(512, 512)
    r["wqir"] = np.ascontiguousarray(wq_p.reshape(4, 128, 4, 128).transpose(1, 0, 2, 3))
    r["wkir"] = np.ascontiguousarray(wk_p.reshape(4, 128, 4, 128).transpose(1, 0, 2, 3))
    r["wvir"] = np.ascontiguousarray(w["L0_ffn_wv"].reshape(4, 128, 4, 128).transpose(1, 0, 2, 3))
    r["wupr"] = np.ascontiguousarray(w["L0_ffn_w_up"].reshape(4, 128, 8, 128).transpose(1, 0, 2, 3))
    r["routerr"] = np.ascontiguousarray(w["L0_ffn_router_w"].reshape(8, 128, 8).transpose(1, 0, 2))
    r["w1r"] = np.ascontiguousarray(w["L0_ffn_w1"].reshape(8, 8, 128, 16, 128).transpose(0, 3, 2, 1, 4))
    r["w3r"] = np.ascontiguousarray(w["L0_ffn_w3"].reshape(8, 8, 128, 16, 128).transpose(0, 3, 2, 1, 4))
    r["w2r"] = np.ascontiguousarray(w["L0_ffn_w2"].reshape(8, 16, 128, 1024).transpose(0, 2, 1, 3))
    r["g1fm"] = np.ascontiguousarray(w["L0_ffn_g1"].reshape(8, 128).T)
    r["g2fm"] = np.ascontiguousarray(w["L0_ffn_g2"].reshape(8, 128).T)
    return r


class Sem:
    def __init__(self, nc, stack, name):
        self.h = stack.enter_context(nc.semaphore(name))
        self.val = 0


class Builder:
    def __init__(self, nc, st):
        self.nc = nc
        self.ops = {e: [] for e in ENGINES}
        self.sem = {e: Sem(nc, st, "sem_" + e) for e in ENGINES}
        self.sem["gdma"] = Sem(nc, st, "sem_gdma")
        for nm in ("dTA0", "dTA1", "dTB0", "dTB1", "gW13", "gW2", "gTS",
                   "gOB", "gSA", "gSB"):
            self.sem[nm] = Sem(nc, st, "sem_" + nm)
        self.stacks = []

    def push(self):
        self.stacks.append(ExitStack())

    def pop(self):
        self.stacks.pop().close()

    def sbuf(self, name, shape, dtype):
        return self.stacks[-1].enter_context(self.nc.sbuf_tensor(name, shape, dtype))

    def psum(self, name, shape, dtype=F32):
        return self.stacks[-1].enter_context(self.nc.psum_tensor(name, shape, dtype))

    def op(self, eng, fn, wait=None):
        s = self.sem[eng]
        inc_n = 16 if eng == "sync" else 1
        s.val += inc_n
        tgt = s.val
        wl = [w for w in (wait or []) if w is not None]
        if eng in ("scalar", "vector"):
            wl.insert(0, (s, tgt - inc_n))

        def closure(e, fn=fn, wl=wl, inc_n=inc_n, s=s):
            for (ws, wv) in wl:
                e.wait_ge(ws.h, wv)
            fn(e).then_inc(s.h, inc_n)
        self.ops[eng].append(closure)
        return (s, tgt)

    def ld(self, out_ap, in_ap, wait=None):
        return self.op("sync", lambda e, o=out_ap, i=in_ap: e.dma_start(out=o, in_=i), wait)

    def ld_on(self, semname, out_ap, in_ap, wait=None):
        s = self.sem[semname]
        s.val += 16
        tgt = s.val
        wl = [w for w in (wait or []) if w is not None]
        wl.insert(0, (s, tgt - 16))

        def closure(e, o=out_ap, i=in_ap, wl=wl, s=s):
            for (ws, wv) in wl:
                e.wait_ge(ws.h, wv)
            e.dma_start(out=o, in_=i).then_inc(s.h, 16)
        self.ops["sync"].append(closure)
        return (s, tgt)

    def gld(self, out_ap, in_ap, wait=None):
        s = self.sem["gdma"]
        s.val += 16
        tgt = s.val
        wl = [w for w in (wait or []) if w is not None]

        def closure(e, o=out_ap, i=in_ap, wl=wl, s=s):
            for (ws, wv) in wl:
                e.wait_ge(ws.h, wv)
            e.dma_start(out=o, in_=i).then_inc(s.h, 16)
        self.ops["gpsimd"].append(closure)
        return (s, tgt)

    def gld_on(self, semname, out_ap, in_ap, wait=None):
        s = self.sem[semname]
        s.val += 16
        tgt = s.val
        wl = [w for w in (wait or []) if w is not None]

        def closure(e, o=out_ap, i=in_ap, wl=wl, s=s):
            for (ws, wv) in wl:
                e.wait_ge(ws.h, wv)
            e.dma_start(out=o, in_=i).then_inc(s.h, 16)
        self.ops["gpsimd"].append(closure)
        return (s, tgt)

    def gop(self, fn, wait=None):
        s = self.sem["gpsimd"]
        s.val += 1
        tgt = s.val
        wl = [w for w in (wait or []) if w is not None]
        wl.insert(0, (s, tgt - 1))

        def closure(e, fn=fn, wl=wl, s=s):
            for (ws, wv) in wl:
                e.wait_ge(ws.h, wv)
            fn(e).then_inc(s.h, 1)
        self.ops["gpsimd"].append(closure)
        return (s, tgt)

    def barrier(self, skip_gw=False):
        targets = {}
        for e in ENGINES:
            s = self.sem[e]
            inc_n = 16 if e == "sync" else 1
            own_prev = s.val
            s.val += inc_n
            targets[e] = (s.val, own_prev, inc_n)
        for e in ENGINES:
            s = self.sem[e]
            tgt, own_prev, inc_n = targets[e]

            gdma_val = self.sem["gdma"].val
            dvals = {nm: self.sem[nm].val for nm in ("dTA0", "dTA1", "dTB0", "dTB1")}
            gvals = ({} if skip_gw else
                     {nm: self.sem[nm].val
                      for nm in ("gW13", "gW2", "gSA", "gSB")})

            def closure(eng, e=e, s=s, own_prev=own_prev, inc_n=inc_n,
                        targets=dict(targets), gdma_val=gdma_val, dvals=dvals):
                if e in ("sync", "gpsimd"):
                    eng.wait_ge(s.h, own_prev)
                if e == "gpsimd":
                    eng.wait_ge(self.sem["gdma"].h, gdma_val)
                    for nm, v in gvals.items():
                        eng.wait_ge(self.sem[nm].h, v)
                if e == "sync":
                    for nm, v in dvals.items():
                        eng.wait_ge(self.sem[nm].h, v)
                if e == "sync":
                    eng.nop().then_inc(s.h, inc_n)
                else:
                    eng.drain().then_inc(s.h, inc_n)
                for o in ENGINES:
                    if o != e:
                        eng.wait_ge(self.sem[o].h, targets[o][0])
            self.ops[e].append(closure)


def cpt(ap, c):
    return ap.rearrange("p (c t) -> p c t", c=c)


def build_program(taps=(), upto=9):
    nc = bass.Bass("TRN2", target_bir_lowering=False)
    st = ExitStack()
    b = Builder(nc, st)

    xh = nc.dram_tensor("x", [T, D], F32, kind="ExternalInput").ap()
    outh = nc.dram_tensor("out", [T, D], F32, kind="ExternalOutput").ap()

    din = {}
    for nm, shape, dt in [
        ("ropeA_o", [128, T], F32), ("ropeB_o", [128, T], F32),
        ("ropeA_i", [128, T], F32), ("ropeB_i", [128, T], F32),
        ("masks", [4, 128, 512], BF16),
        ("swapO", [128, 128], BF16), ("swapI", [128, 128], BF16),
        ("identB", [128, 128], BF16), ("identF", [128, 128], F32),
        ("onesB", [128, 128], BF16), ("onesF", [128, 128], F32),
        ("epat", [8, 1024], BF16), ("epsc", [128, 1], F32),
        ("wqr", [8, 128, 8, 128], F32), ("wkr", [8, 128, 8, 128], F32),
        ("wvr", [8, 128, 8, 128], F32), ("wor", [8, 128, 1024], F32),
        ("wdownr", [128, 8, 4, 128], F32),
        ("wqir", [128, 4, 4, 128], F32), ("wkir", [128, 4, 4, 128], F32),
        ("wvir", [128, 4, 4, 128], F32), ("wupr", [128, 4, 8, 128], F32),
        ("routerr", [128, 8, 8], F32),
        ("w1r", [8, 16, 128, 8, 128], F32), ("w3r", [8, 16, 128, 8, 128], F32),
        ("w2r", [8, 128, 16, 1024], F32),
        ("g1fm", [128, 8], F32), ("g2fm", [128, 8], F32),
    ]:
        din[nm] = nc.dram_tensor(nm, shape, dt, kind="ExternalInput").ap()

    def scratch(nm, shape, dt):
        kind = "ExternalOutput" if nm in taps else "Internal"
        return nc.dram_tensor(nm, shape, dt, kind=kind).ap()

    xT_s = scratch("xT_s", [8, 128, T], F32)
    x1_s = scratch("x1_s", [8, 128, T], F32)
    ln3_s = scratch("ln3_s", [8, 128, T], F32)
    xf_s = scratch("xf_s", [8, 128, T], F32)
    h_s = scratch("h_s", [8, 128, T], BF16)
    t_s = scratch("t_s", [8, 128, T], BF16)
    ob_s = scratch("ob_s", [8, 128, T], BF16)
    c_s = scratch("c_s", [4, 128, T], BF16)
    o2_s = scratch("o2_s", [4, 128, T], BF16)
    gate_s = scratch("gate_s", [8, T], BF16)
    y_s = scratch("y_s", [8, 128, T], F32)
    qkv_s = scratch("qkv_s", [3, 128, T], BF16)

    b.push()
    C_swapO = b.sbuf("c_swapO", [128, 128], BF16)
    C_swapI = b.sbuf("c_swapI", [128, 128], BF16)
    C_identB = b.sbuf("c_identB", [128, 128], BF16)
    C_identF = b.sbuf("c_identF", [128, 128], F32)
    C_onesB = b.sbuf("c_onesB", [128, 128], BF16)
    C_onesF = b.sbuf("c_onesF", [128, 128], F32)
    C_epat = b.sbuf("c_epat", [8, 1024], BF16)
    C_eps = b.sbuf("c_eps", [128, 1], F32)
    C_g1 = b.sbuf("c_g1", [128, 8], F32)
    C_g2 = b.sbuf("c_g2", [128, 8], F32)
    GATE = b.sbuf("gate_fm", [8, T], BF16)
    SROW = b.sbuf("srow", [1, 1536], F32)
    STAT = b.sbuf("stat", [128, 16], F32)
    CBIAS = b.sbuf("cbias", [128, 8], F32)
    RECROW = b.sbuf("recrow", [128, 512], F32)
    BCAST = b.sbuf("bcastt", [128, 512], F32)

    for nm, dst in [("swapO", C_swapO),
                    ("swapI", C_swapI), ("identB", C_identB), ("identF", C_identF),
                    ("onesB", C_onesB), ("onesF", C_onesF), ("epat", C_epat),
                    ("epsc", C_eps),
                    ("g1fm", C_g1), ("g2fm", C_g2)]:
        if nm in ("masks", "masks4"):
            b.ld(dst[:].rearrange("p (j c) -> p j c", j=4),
                 din[nm].rearrange("j p c -> p j c"))
        else:
            b.ld(dst[:], din[nm])
    b.barrier()

    b.push()
    TABA = b.sbuf("taba", [128, 1024], F32)
    TABB = b.sbuf("tabb", [128, 1024], F32)
    C_masks = b.sbuf("c_masks", [128, 2048], BF16)
    b.ld(C_masks[:].rearrange("p (j c) -> p j c", j=4),
         din["masks"].rearrange("j p c -> p j c"))
    b.push()
    b.push()
    LN1 = b.sbuf("ln1", [128, 8 * T], BF16)

    b.push()
    P1X = b.sbuf("p1_x", [128, 2048], F32)
    P1XC = b.sbuf("p1_xc", [128, 1024], F32)
    P1SQ = b.sbuf("p1_sq", [128, 1024], F32)
    P1LN = b.sbuf("p1_lnt", [128, 1024], F32)
    P1ST = b.sbuf("p1_xstg", [128, 2048], F32)
    PS_L1 = b.psum("p1_psl", [128, 1024], F32)
    PS_X1 = b.psum("p1_psx", [128, 1024], F32)

    x_guard = [None, None]
    st_dma = [None, None]
    prev = {"sq": None, "tp": None, "evl": None, "evx": None, "rstd": None}
    for it in range(32):
        sl = it % 2
        xsl = P1X[:, 1024 * sl:1024 * (sl + 1)]
        t_load = b.ld_on(f"dTB{sl}", xsl, xh[128 * it:128 * (it + 1), :],
                         wait=[x_guard[sl]])
        b.op("vector", lambda e, xsl=xsl: e.tensor_reduce(
            STAT[:, 0:1], xsl, axis=AX.X, op=ADD), wait=[t_load])
        b.op("vector", lambda e: e.tensor_scalar(
            STAT[:, 1:2], STAT[:, 0:1], 1.0 / D, None, op0=MULT))
        t_xc = b.op("vector", lambda e, xsl=xsl: e.tensor_scalar(
            P1XC[:], xsl, STAT[:, 1:2], None, op0=SUB), wait=[prev["sq"]])
        t_sq = b.op("scalar", lambda e: e.activation(
            P1SQ[:], P1XC[:], AF.Square, accum_out=STAT[:, 2:3]), wait=[t_xc])
        t_std = b.op("scalar", lambda e: e.activation(
            STAT[:, 3:4], STAT[:, 2:3], AF.Sqrt, bias=C_eps[:, 0:1],
            scale=1.0 / D), wait=[prev["rstd"]])
        t_rstd = b.op("vector", lambda e: e.reciprocal(
            STAT[:, 4:5], STAT[:, 3:4]), wait=[t_std])
        t_ln = b.op("vector", lambda e: e.tensor_scalar(
            P1LN[:], P1XC[:], STAT[:, 4:5], None, op0=MULT), wait=[prev["tp"]])
        prev["sq"] = t_sq
        prev["rstd"] = t_rstd
        tp = None
        for c in range(8):
            wv = [t_ln] if c == 0 else None
            if c == 0 and prev["evl"] is not None:
                wv = [t_ln, prev["evl"]]
            tp = b.op("tensor", lambda e, c=c: e.transpose(
                PS_L1[:, 128 * c:128 * (c + 1)], P1LN[:, 128 * c:128 * (c + 1)],
                C_identF[:]), wait=wv)
        prev["tp"] = tp
        tpx = None
        for c in range(8):
            wv = None
            if c == 0 and prev["evx"] is not None:
                wv = [prev["evx"]]
            tpx = b.op("tensor", lambda e, c=c, xsl=xsl: e.transpose(
                PS_X1[:, 128 * c:128 * (c + 1)], xsl[:, 128 * c:128 * (c + 1)],
                C_identF[:]), wait=wv)
        x_guard[sl] = tpx
        t_evl = b.op("scalar", lambda e, it=it: e.copy(
            cpt(LN1[:], 8)[:, :, 128 * it:128 * (it + 1)], cpt(PS_L1[:], 8)),
            wait=[tp])
        prev["evl"] = t_evl
        stg = P1ST[:, 1024 * sl:1024 * (sl + 1)]
        t_evx = b.op("scalar", lambda e, stg=stg: e.copy(stg, PS_X1[:]),
                     wait=[tpx, st_dma[sl]])
        prev["evx"] = t_evx
        st_dma[sl] = b.gld_on(f"gS{'AB'[sl]}",
                              xT_s[:, :, 128 * it:128 * (it + 1)]
                              .rearrange("c p t -> p c t"),
                              cpt(stg, 8), wait=[t_evx])
    b.barrier()
    b.pop()

    if upto < 2:
        return finish(nc, st, b)

    b.push()
    QKV2 = b.sbuf("p2_qkv", [128, 6 * T], BF16)
    WS2 = b.sbuf("p2_ws", [128, 6144], BF16)
    SQB2 = b.sbuf("p2_sqb", [128, 512], BF16)
    TMPG = b.sbuf("p2_tmpg", [128, 1024], F32)
    TMPB = b.sbuf("p2_tmpb", [128, 1024], F32)
    QSW = b.sbuf("p2_qsw", [128, 2 * T], BF16)
    PBUF = b.sbuf("p2_pbuf", [128, 2048], BF16)
    OSTH = b.sbuf("p2_osth", [128, 1024], BF16)
    PS_P2 = b.psum("p2_proj", [128, 1024], F32)
    PS_W2 = b.psum("p2_swap", [128, 512], F32)
    PS_S2 = b.psum("p2_S", [128, 2048], F32)
    PS_O2 = b.psum("p2_O", [128, 512], F32)
    PS_L2 = PS_W2
    PS_N2 = PS_W2

    g2s = {"bcev": None, "cbias": None, "osg": None,
           "pguard": [None, None], "lguard": None, "exphist": [None, None],
           "odma": [None, None], "swev": None, "bvlast": None}

    def cauchy_norms(buf, region_off, statcol, init=None, fins=None):
        last = init
        for tt in range(8):
            wvs = [last]
            if fins is not None:
                wvs.append(fins[tt])
            t_sq = b.op("scalar", lambda e, ro=region_off, tt=tt, buf=buf:
                        e.activation(
                SQB2[:], buf[:, ro + 512 * tt:ro + 512 * tt + 512], AF.Square),
                wait=wvs)
            wvm = [t_sq, last]
            if tt == 0:
                wvm.append(g2s["bcev"])
            t_mm = b.op("tensor", lambda e: e.matmul(
                PS_N2[0:1, :], C_onesB[:, 0:1], SQB2[:], start=True, stop=True),
                wait=wvm)
            t_mx = b.op("vector", lambda e: e.reduce_max(
                STAT[0:1, 14:15], PS_N2[0:1, :], axis=AX.X), wait=[t_mm])
            if tt == 0:
                last = b.op("vector", lambda e, sc=statcol: e.tensor_copy(
                    STAT[0:1, sc:sc + 1], STAT[0:1, 14:15]),
                    wait=[g2s["cbias"]])
            else:
                last = b.op("vector", lambda e, sc=statcol: e.tensor_tensor(
                    STAT[0:1, sc:sc + 1], STAT[0:1, sc:sc + 1], STAT[0:1, 14:15],
                    op=AluOpType.max))
        return last

    def bias_from_norms(psN, scale, biascol, wait):
        t1 = b.op("scalar", lambda e: e.activation(
            STAT[0:1, 10:11], STAT[0:1, 8:9], AF.Sqrt), wait=wait)
        t2 = b.op("scalar", lambda e: e.activation(
            STAT[0:1, 11:12], STAT[0:1, 9:10], AF.Sqrt))
        t3 = b.op("vector", lambda e: e.tensor_tensor(
            STAT[0:1, 12:13], STAT[0:1, 10:11], STAT[0:1, 11:12], op=MULT),
            wait=[t1, t2])
        t4 = b.op("vector", lambda e, s=scale: e.tensor_scalar(
            STAT[0:1, 13:14], STAT[0:1, 12:13], -float(s), None, op0=MULT))
        t5 = b.op("tensor", lambda e, psN=psN: e.matmul(
            psN[:, 0:1], C_onesF[0:1, :], STAT[0:1, 13:14], start=True, stop=True),
            wait=[t4, g2s["bcev"]])
        t6 = b.op("scalar", lambda e, bc=biascol, psN=psN: e.copy(
            CBIAS[:, bc:bc + 1], psN[:, 0:1]), wait=[t5])
        g2s["cbias"] = t6
        return t6

    rope_state = {"s0": None, "s1": None, "g0": None, "g1": None}

    def rope_combine(buf, reg, swbuf, swreg, tabA_d, tabB_d, nchunk_tiles,
                     sw_evs, tmpG, tmpB, evs=None):
        prevs = [rope_state["s0"], rope_state["s1"]]
        prevg = [rope_state["g0"], rope_state["g1"]]
        fins = []
        for tt in range(nchunk_tiles):
            sl = tt % 2
            tA = b.ld_on(f"dTA{sl}", TABA[:, 512 * sl:512 * (sl + 1)],
                         tabA_d[:, 512 * (tt % 8):512 * (tt % 8) + 512],
                         wait=[prevs[sl]])
            tB = b.ld_on(f"dTB{sl}", TABB[:, 512 * sl:512 * (sl + 1)],
                         tabB_d[:, 512 * (tt % 8):512 * (tt % 8) + 512],
                         wait=[prevs[sl]])
            sl_ap = buf[:, reg + 512 * tt:reg + 512 * tt + 512]
            sw_ap = swbuf[:, swreg + 512 * tt:swreg + 512 * tt + 512]
            t_a = b.gop(lambda e, s=sl_ap, sl=sl, tmpG=tmpG: e.tensor_tensor(
                tmpG[:, 512 * sl:512 * (sl + 1)], s,
                TABA[:, 512 * sl:512 * (sl + 1)], op=MULT),
                wait=[tA, prevs[sl],
                      evs[tt] if evs is not None else None])
            t_bv = b.op("vector", lambda e, sw=sw_ap, sl=sl, tmpB=tmpB:
                        e.tensor_tensor(
                tmpB[:, 512 * sl:512 * (sl + 1)], sw,
                TABB[:, 512 * sl:512 * (sl + 1)], op=MULT),
                wait=[sw_evs[tt], tB, prevg[sl]])
            t_fin = b.op("vector", lambda e, s=sl_ap, sl=sl, tmpG=tmpG,
                         tmpB=tmpB: e.tensor_tensor(
                s, tmpG[:, 512 * sl:512 * (sl + 1)],
                tmpB[:, 512 * sl:512 * (sl + 1)], op=ADD), wait=[t_a])
            prevs[sl] = t_fin
            prevg[sl] = t_fin
            fins.append(t_fin)
        rope_state["s0"], rope_state["s1"] = prevs
        rope_state["g0"], rope_state["g1"] = prevg
        return fins

    wsem = ["gW13", "gW2"]
    w_tgt = [None, None]
    w_slot_free = [None, None]

    def load_head(h):
        slot = h % 2
        base = 3072 * slot
        sm = wsem[slot]
        t = b.gld_on(sm, WS2[:, base:base + 1024],
                     din["wqr"][h].rearrange("p c f -> p (c f)"),
                     wait=[w_slot_free[slot]])
        t = b.gld_on(sm, WS2[:, base + 1024:base + 2048],
                     din["wkr"][h].rearrange("p c f -> p (c f)"))
        t = b.gld_on(sm, WS2[:, base + 2048:base + 3072],
                     din["wvr"][h].rearrange("p c f -> p (c f)"))
        w_tgt[slot] = t

    hs = {}

    def emit_proj(h):
        slot = h % 2
        wb0 = 3072 * slot
        qb = 3 * T * slot
        st = {"qk_evs": {0: [None] * 8, T: [None] * 8}, "v_evs": [None] * 32}
        prev_ev2 = list(hs[h - 1]["pev"]) if h >= 1 else [None, None]
        first = True
        free_ev = hs[h - 2]["slot_free"] if h >= 2 else None
        for (reg, wb) in ((0, wb0), (T, wb0 + 1024)):
            for tt in range(8):
                par = tt % 2
                psb = PS_P2[:, 512 * par:512 * (par + 1)]
                mm = None
                for c in range(8):
                    wv = None
                    if c == 0:
                        wv = [prev_ev2[par]]
                        if first:
                            wv.append((b.sem[wsem[slot]], w_tgt[slot][1]))
                            first = False
                    mm = b.op("tensor", lambda e, c=c, tt=tt, wb=wb, psb=psb: e.matmul(
                        psb, WS2[:, wb + 128 * c:wb + 128 * (c + 1)],
                        LN1[:, T * c + 512 * tt:T * c + 512 * tt + 512],
                        start=(c == 0), stop=(c == 7)), wait=wv)
                wv = [mm]
                if reg == 0 and tt == 0:
                    wv.append(free_ev)
                ev = b.op("scalar", lambda e, reg=reg, tt=tt, psb=psb, qb=qb: e.copy(
                    QKV2[:, qb + reg + 512 * tt:qb + reg + 512 * tt + 512], psb),
                    wait=wv)
                st["qk_evs"][reg][tt] = ev
                prev_ev2[par] = ev
        lastVmm = None
        for tc in range(32):
            par = tc % 2
            psbv = PS_P2[:, 512 * par:512 * par + 128]
            mm = None
            for c in range(8):
                mm = b.op("tensor", lambda e, c=c, tc=tc, psbv=psbv, wb0=wb0: e.matmul(
                    psbv,
                    LN1[:, T * c + 128 * tc:T * c + 128 * (tc + 1)],
                    WS2[:, wb0 + 2048 + 128 * c:wb0 + 2048 + 128 * (c + 1)],
                    start=(c == 0), stop=(c == 7)),
                    wait=[prev_ev2[par]] if c == 0 else None)
            if tc == 31:
                lastVmm = mm
            ev = b.op("scalar", lambda e, tc=tc, psbv=psbv, qb=qb: e.copy(
                QKV2[:, qb + 2 * T + 128 * tc:qb + 2 * T + 128 * (tc + 1)], psbv),
                wait=[mm])
            st["v_evs"][tc] = ev
            prev_ev2[par] = ev
        swA = PS_W2[:]
        sw_evs = {0: [None] * 8, T: [None] * 8}
        swev = g2s["swev"]
        first_sw = True
        for (reg, _) in ((0, wb0), (T, wb0 + 1024)):
            for tt in range(8):
                wv = [st["qk_evs"][reg][tt], swev]
                if first_sw:
                    wv += [g2s["bvlast"], g2s["cbias"]]
                    first_sw = False
                t_sw = b.op("tensor", lambda e, reg=reg, tt=tt, qb=qb, swA=swA:
                            e.matmul(
                    swA, C_swapO[:],
                    QKV2[:, qb + reg + 512 * tt:qb + reg + 512 * tt + 512],
                    start=True, stop=True), wait=wv)
                swev = b.op("scalar", lambda e, reg=reg, tt=tt, swA=swA: e.copy(
                    QSW[:, reg + 512 * tt:reg + 512 * tt + 512], swA),
                    wait=[t_sw])
                sw_evs[reg][tt] = swev
        g2s["swev"] = swev
        st["wfree"] = lastVmm
        st["v_done"] = st["v_evs"][31]
        st["pev"] = list(prev_ev2)
        st["sw_evs"] = sw_evs
        hs[h] = st

    def emit_rope(h):
        slot = h % 2
        qb = 3 * T * slot
        st = hs[h]
        st["finsq"] = rope_combine(QKV2, qb, QSW, 0, din["ropeA_o"],
                                   din["ropeB_o"], 8, st["sw_evs"][0],
                                   TMPG, TMPB, evs=st["qk_evs"][0])
        st["finsk"] = rope_combine(QKV2, qb + T, QSW, T, din["ropeA_o"],
                                   din["ropeB_o"], 8, st["sw_evs"][T],
                                   TMPG, TMPB, evs=st["qk_evs"][T])
        g2s["bvlast"] = st["finsk"][7]

    def emit_bound(h):
        slot = h % 2
        qb = 3 * T * slot
        st = hs[h]
        tqn = cauchy_norms(QKV2, qb, 8, fins=st["finsq"])
        tkn = cauchy_norms(QKV2, qb + T, 9, init=tqn, fins=st["finsk"])
        st["bias"] = bias_from_norms(PS_N2, ISC_O, slot, [tqn, tkn])

    def emit_attn(h):
        slot = h % 2
        qb = 3 * T * slot
        st = hs[h]
        p_guard = g2s["pguard"]
        ps_o_guard = g2s["osg"]
        ps_l_guard = g2s["lguard"]
        exp_hist = g2s["exphist"]
        prev_odma = g2s["odma"]
        last_pv = None
        for qt in range(8):
            nkc = 4 * qt + 4
            npair = nkc // 2
            last_l = None
            tps = {}
            for p in range(npair + 1):
                if p < npair:
                    psl = p % 2
                    sb = p % 2
                    mms = None
                    for half in range(2):
                        kc = 2 * p + half
                        wv = None
                        if half == 0:
                            wv = [exp_hist[sb]]
                            if p == 0:
                                wv += [st["finsq"][qt], st["finsk"][qt]]
                                if qt == 0:
                                    wv.append(st["v_done"])
                        mms = b.op("tensor", lambda e, kc=kc, qt=qt, sb=sb,
                                   half=half, qb=qb: e.matmul(
                            PS_S2[:, 1024 * sb + 512 * half:
                                  1024 * sb + 512 * (half + 1)],
                            QKV2[:, qb + T + 128 * kc:qb + T + 128 * (kc + 1)],
                            QKV2[:, qb + 512 * qt:qb + 512 * (qt + 1)],
                            start=True, stop=True), wait=wv)
                    t_exp = b.op("scalar", lambda e, psl=psl, sb=sb, slot=slot:
                                 e.activation(
                        PBUF[:, 1024 * psl:1024 * (psl + 1)],
                        PS_S2[:, 1024 * sb:1024 * (sb + 1)], AF.Exp,
                        bias=CBIAS[:, slot:slot + 1], scale=float(ISC_O)),
                        wait=[mms, p_guard[psl], st["bias"]])
                    exp_hist[sb] = t_exp
                    t_p = t_exp
                    for half in range(2):
                        kc = 2 * p + half
                        j = kc - 4 * qt
                        if j >= 0:
                            t_p = b.op("vector", lambda e, psl=psl, j=j,
                                       half=half: e.tensor_tensor(
                                PBUF[:, 1024 * psl + 512 * half:
                                     1024 * psl + 512 * (half + 1)],
                                PBUF[:, 1024 * psl + 512 * half:
                                     1024 * psl + 512 * (half + 1)],
                                C_masks[:, 512 * j:512 * (j + 1)], op=MULT),
                                wait=[t_exp])
                    tps[p] = t_p
                pp = p - 1
                if pp >= 0:
                    psl = pp % 2
                    for half in range(2):
                        kc = 2 * pp + half
                        wv = [tps[pp]] if half == 0 else None
                        if kc == 0:
                            wv = [tps[pp], ps_o_guard]
                        last_pv = b.op("tensor", lambda e, kc=kc, psl=psl,
                                       half=half, nkc=nkc, qb=qb:
                                       e.matmul(
                            PS_O2[:],
                            QKV2[:, qb + 2 * T + 128 * kc:qb + 2 * T + 128 * (kc + 1)],
                            PBUF[:, 1024 * psl + 512 * half:
                                 1024 * psl + 512 * (half + 1)],
                            start=(kc == 0), stop=(kc == nkc - 1)), wait=wv)
                        wv2 = [ps_l_guard] if kc == 0 else None
                        last_l = b.op("tensor", lambda e, kc=kc, psl=psl,
                                      half=half, nkc=nkc: e.matmul(
                            PS_L2[0:1, :], C_onesB[:, 0:1],
                            PBUF[:, 1024 * psl + 512 * half:
                                 1024 * psl + 512 * (half + 1)],
                            start=(kc == 0), stop=(kc == nkc - 1)), wait=wv2)
                    p_guard[psl] = last_l
            t_rec = b.op("vector", lambda e: e.reciprocal(
                RECROW[0:1, :], PS_L2[0:1, :]), wait=[last_l])
            ps_l_guard = None
            t_bc = b.op("tensor", lambda e: e.matmul(
                PS_N2[:], C_onesF[0:1, :], RECROW[0:1, :], start=True, stop=True),
                wait=[t_rec, g2s["cbias"], g2s["swev"]])
            t_bcev = b.op("scalar", lambda e: e.copy(BCAST[:], PS_N2[:]),
                          wait=[t_bc, ps_o_guard])
            g2s["bcev"] = t_bcev
            ps_l_guard = t_bcev
            osl = qt % 2
            ps_o_guard = b.op("vector", lambda e, osl=osl: e.tensor_tensor(
                OSTH[:, 512 * osl:512 * (osl + 1)],
                PS_O2[:], BCAST[:], op=MULT),
                wait=[t_bcev, last_pv, prev_odma[osl]])
            prev_odma[osl] = b.ld(ob_s[h][:, 512 * qt:512 * qt + 512],
                                  OSTH[:, 512 * osl:512 * (osl + 1)],
                                  wait=[ps_o_guard])
        g2s["osg"] = ps_o_guard
        g2s["pguard"] = p_guard
        g2s["lguard"] = ps_l_guard
        g2s["exphist"] = exp_hist
        g2s["odma"] = prev_odma
        st["slot_free"] = last_pv

    load_head(0)
    for h in range(9):
        if h < 8:
            if h >= 1:
                w_slot_free[(h + 1) % 2] = hs[h - 1]["wfree"]
            if h < 7:
                load_head(h + 1)
            emit_proj(h)
        if h < 8:
            emit_rope(h)
        if h == 0:
            emit_bound(0)
        if h >= 1:
            emit_attn(h - 1)
            if h < 8:
                emit_bound(h)
    b.barrier()
    if "qkv_s" in taps:
        for r in range(3):
            b.ld(qkv_s[r], QKV2[:, 3 * T + T * r:3 * T + T * (r + 1)])
        b.gld(y_s[0], QSW[:, 0:T])
        b.gld(y_s[1], QSW[:, T:2 * T])
        b.barrier()
    b.pop()
    b.pop()

    if upto < 3:
        return finish(nc, st, b)

    b.push()
    WO3 = b.sbuf("p3_wo", [128, 8192], BF16)
    OB3 = b.sbuf("p3_ob", [128, 8192], BF16)
    FPA = b.sbuf("p3_fpa", [128, 4096], F32)
    FPD = b.sbuf("p3_fpd", [128, 4096], F32)
    FPB = b.sbuf("p3_fpb", [128, 4096], F32)
    FPC = b.sbuf("p3_fpc", [128, 4096], F32)
    XRES = b.sbuf("p3_xres", [128, 4096], F32)
    BQ3 = b.sbuf("p3_bq", [128, 4096], BF16)
    PS_A3 = b.psum("p3_a", [128, 512], F32)
    PS_M3 = b.psum("p3_m", [128, 512], F32)
    PS_B3 = b.psum("p3_b", [128, 512], F32)

    for hh in range(8):
        b.gld(WO3[:, 1024 * hh:1024 * (hh + 1)], din["wor"][hh])
    b.barrier()

    def stats_mean(src_bf, psM, psB, wait):
        mm = None
        for m in range(8):
            mm = b.op("tensor", lambda e, m=m, psM=psM: e.matmul(
                psM[0:1, :], C_onesB[:, 0:1], src_bf[:, 512 * m:512 * (m + 1)],
                start=(m == 0), stop=(m == 7)), wait=wait if m == 0 else None)
        t_r = b.op("vector", lambda e, psM=psM: e.tensor_scalar(
            SROW[0:1, 512:1024], psM[0:1, :], 1.0 / D, None, op0=MULT),
            wait=[mm])
        t_b = b.op("tensor", lambda e, psB=psB: e.matmul(
            psB[:], C_onesF[0:1, :], SROW[0:1, 512:1024], start=True, stop=True),
            wait=[t_r])
        return b.op("scalar", lambda e, psB=psB: e.copy(BCAST[:], psB[:]),
                    wait=[t_b])

    def stats_rstd(src_bf, psM, psB, wait):
        mm = None
        for m in range(8):
            mm = b.op("tensor", lambda e, m=m, psM=psM: e.matmul(
                psM[0:1, :], C_onesB[:, 0:1], src_bf[:, 512 * m:512 * (m + 1)],
                start=(m == 0), stop=(m == 7)), wait=wait if m == 0 else None)
        t_srt = b.op("scalar", lambda e, psM=psM: e.activation(
            SROW[0:1, 512:1024], psM[0:1, :], AF.Sqrt, bias=C_eps[0:1, 0:1], scale=1.0 / D),
            wait=[mm])
        t_rc = b.op("vector", lambda e: e.reciprocal(
            SROW[0:1, 1024:1536], SROW[0:1, 512:1024]), wait=[t_srt])
        t_b = b.op("tensor", lambda e, psB=psB: e.matmul(
            psB[:], C_onesF[0:1, :], SROW[0:1, 1024:1536], start=True, stop=True),
            wait=[t_rc])
        return b.op("scalar", lambda e, psB=psB: e.copy(BCAST[:], psB[:]),
                    wait=[t_b])

    def layernorm_block(Xsrc, Xdst, SQbf, psM, psB, wait):
        t_cp = None
        for m in range(8):
            t_cp = b.op("scalar", lambda e, m=m: e.copy(
                SQbf[:, 512 * m:512 * (m + 1)], Xsrc[:, 512 * m:512 * (m + 1)]),
                wait=wait if m == 0 else None)
        t_mb = stats_mean(SQbf, psM, psB, [t_cp])
        t_c = None
        for m in range(8):
            t_c = b.op("vector", lambda e, m=m: e.tensor_tensor(
                FPC[:, 512 * m:512 * (m + 1)], Xsrc[:, 512 * m:512 * (m + 1)],
                BCAST[:], op=SUB), wait=[t_mb] if m == 0 else None)
        t_sq = None
        for m in range(8):
            t_sq = b.op("scalar", lambda e, m=m: e.activation(
                SQbf[:, 512 * m:512 * (m + 1)], FPC[:, 512 * m:512 * (m + 1)],
                AF.Square), wait=[t_c] if m == 0 else None)
        t_rb = stats_rstd(SQbf, psM, psB, [t_sq])
        t_o = None
        for m in range(8):
            t_o = b.op("vector", lambda e, m=m: e.tensor_tensor(
                Xdst[:, 512 * m:512 * (m + 1)], FPC[:, 512 * m:512 * (m + 1)],
                BCAST[:], op=MULT), wait=[t_rb] if m == 0 else None)
        return t_o

    g3 = {"t_h": [None, None], "t_x1": None, "x1dma": None, "hdma": None,
          "ln3dma": [None, None], "obmm": [None, None], "obdma": [None, None]}

    def load_ob(blk):
        par = blk % 2
        return b.ld_on("gOB", cpt(OB3[:, 4096 * par:4096 * par + 4096], 8),
                       ob_s[:, :, 512 * blk:512 * blk + 512]
                       .rearrange("h p t -> p h t"),
                       wait=[g3["obmm"][par]])

    ob_evs = [load_ob(0), load_ob(1)]
    for blk in range(8):
        par = blk % 2
        obo = 4096 * par
        FP = FPA if par == 0 else FPD
        t_ob = ob_evs[blk]
        prev_ev = None
        mm = None
        for m in range(8):
            for hh in range(8):
                wv = None
                if hh == 0:
                    wv = [prev_ev]
                    if m == 0:
                        wv.append(t_ob)
                mm = b.op("tensor", lambda e, m=m, hh=hh, obo=obo: e.matmul(
                    PS_A3[:], WO3[:, 1024 * hh + 128 * m:1024 * hh + 128 * (m + 1)],
                    OB3[:, obo + 512 * hh:obo + 512 * (hh + 1)],
                    start=(hh == 0), stop=(hh == 7)),
                    wait=wv)
            wv = [mm]
            if m == 0:
                wv += [g3["t_h"][par], g3["ln3dma"][par]]
            prev_ev = b.op("scalar", lambda e, m=m, FP=FP: e.copy(
                FP[:, 512 * m:512 * (m + 1)], PS_A3[:]), wait=wv)
        g3["obmm"][par] = mm
        if blk + 2 < 8:
            ob_evs.append(load_ob(blk + 2))
        t_ln2 = layernorm_block(FP, FP, BQ3, PS_M3, PS_B3,
                                [prev_ev, g3["hdma"]])
        t_xr = b.ld_on("dTB0", cpt(XRES[:], 8),
                       xT_s[:, :, 512 * blk:512 * blk + 512]
                       .rearrange("c p t -> p c t"), wait=[g3["t_x1"]])
        t_x1 = None
        for m in range(8):
            wv = None
            if m == 0:
                wv = [t_ln2, t_xr, g3["x1dma"]]
            t_x1 = b.op("vector", lambda e, m=m, FP=FP: e.tensor_tensor(
                FPB[:, 512 * m:512 * (m + 1)], FP[:, 512 * m:512 * (m + 1)],
                XRES[:, 512 * m:512 * (m + 1)], op=ADD), wait=wv)
        g3["t_x1"] = t_x1
        g3["x1dma"] = b.gld_on(
            "gSA", x1_s[:, :, 512 * blk:512 * blk + 512]
            .rearrange("c p t -> p c t"), cpt(FPB[:], 8), wait=[t_x1])
        t_ln3 = layernorm_block(FPB, FP, BQ3, PS_M3, PS_B3, [t_x1])
        g3["ln3dma"][par] = b.gld_on(
            "gSB", ln3_s[:, :, 512 * blk:512 * blk + 512]
            .rearrange("c p t -> p c t"), cpt(FP[:], 8), wait=[t_ln3])
        t_sq = None
        for m in range(8):
            wv = None
            if m == 0:
                wv = [t_ln3, g3["hdma"]]
            t_sq = b.op("scalar", lambda e, m=m, FP=FP: e.activation(
                BQ3[:, 512 * m:512 * (m + 1)], FP[:, 512 * m:512 * (m + 1)],
                AF.Square), wait=wv)
        t_rb = stats_rstd(BQ3, PS_M3, PS_B3, [t_sq])
        t_h = None
        for m in range(8):
            b.op("vector", lambda e, m=m, FP=FP: e.tensor_tensor(
                FPC[:, 512 * m:512 * (m + 1)], FP[:, 512 * m:512 * (m + 1)],
                BCAST[:], op=MULT), wait=[t_rb] if m == 0 else None)
            t_h = b.op("vector", lambda e, m=m: e.tensor_scalar(
                BQ3[:, 512 * m:512 * (m + 1)], FPC[:, 512 * m:512 * (m + 1)],
                C_g1[:, m:m + 1], None, op0=MULT))
        g3["t_h"][par] = t_h
        g3["hdma"] = b.gld_on(
            "gSA", h_s[:, :, 512 * blk:512 * blk + 512]
            .rearrange("c p t -> p c t"), cpt(BQ3[:], 8), wait=[t_h])
    b.barrier()
    b.pop()
    b.pop()

    if upto < 4:
        return finish(nc, st, b)

    b.push()
    WS4 = b.sbuf("p4_ws", [128, 8192], BF16)
    TMPA4 = b.sbuf("p4_tmpa", [128, 1024], F32)
    TMPB4 = b.sbuf("p4_tmpb", [128, 1024], F32)
    b.push()
    CBUF = b.sbuf("p4_c", [128, 4 * T], BF16)
    QKI = b.sbuf("p4_qk", [128, 8 * T], BF16)
    VBF = b.sbuf("p4_v", [128, 4 * T], BF16)
    HB4 = b.sbuf("p4_hb", [128, 4096], BF16)
    PB4 = b.sbuf("p4_pbuf", [128, 4096], BF16)
    OST4 = b.sbuf("p4_ost", [128, 512], BF16)
    SQB4 = PB4

    b.push()
    PS_S4 = b.psum("p4_S", [128, 2048], F32)
    PS_O4 = b.psum("p4_O", [128, 512], F32)
    PS_L4 = b.psum("p4_l", [128, 512], F32)
    PS_B4 = b.psum("p4_B", [128, 512], F32)

    b.gld(WS4[:, 0:4096], din["wdownr"].rearrange("p d c f -> p (d c f)"))
    b.barrier()
    hb_guard = [None, None]
    prev_ev2 = [None, None]
    for tt in range(8):
        sl = tt % 2
        hoff = 0
        t_hb = b.ld_on(f"dTA{sl}",
                       cpt(HB4[:], 8),
                       h_s[:, :, 512 * tt:512 * tt + 512]
                       .rearrange("c p t -> p c t"), wait=[hb_guard[0], hb_guard[1]])
        for cc in range(4):
            par = cc % 2
            psb = [PS_O4, PS_L4][par]
            mm = None
            for dd in range(8):
                wv = None
                if dd == 0:
                    wv = [t_hb, prev_ev2[par]]
                mm = b.op("tensor", lambda e, cc=cc, dd=dd, hoff=hoff, psb=psb: e.matmul(
                    psb[:], WS4[:, 512 * dd + 128 * cc:512 * dd + 128 * (cc + 1)],
                    HB4[:, hoff + 512 * dd:hoff + 512 * (dd + 1)],
                    start=(dd == 0), stop=(dd == 7)), wait=wv)
            prev_ev2[par] = b.op("scalar", lambda e, cc=cc, tt=tt, psb=psb: e.copy(
                CBUF[:, T * cc + 512 * tt:T * cc + 512 * tt + 512], psb[:]),
                wait=[mm])
        hb_guard[sl] = mm
    b.barrier()
    if "c_s" in taps:
        for c in range(4):
            b.ld(c_s[c], CBUF[:, T * c:T * (c + 1)])
        b.barrier()

    b.gld(WS4[:, 0:2048], din["wqir"].rearrange("p c q f -> p (c q f)"))
    b.gld(WS4[:, 2048:4096], din["wkir"].rearrange("p c q f -> p (c q f)"))
    b.gld(WS4[:, 4096:6144], din["wvir"].rearrange("p c q f -> p (c q f)"))
    b.barrier()
    prev_ev2 = [None, None]
    idx = 0
    for (reg, wb) in ((0, 0), (4 * T, 2048)):
        for qc in range(4):
            for tt in range(8):
                par = idx % 2
                idx += 1
                psb = [PS_O4, PS_L4][par]
                mm = None
                for cc in range(4):
                    mm = b.op("tensor", lambda e, cc=cc, qc=qc, tt=tt, wb=wb,
                              psb=psb: e.matmul(
                        psb[:],
                        WS4[:, wb + 512 * cc + 128 * qc:wb + 512 * cc + 128 * (qc + 1)],
                        CBUF[:, T * cc + 512 * tt:T * cc + 512 * tt + 512],
                        start=(cc == 0), stop=(cc == 3)),
                        wait=[prev_ev2[par]] if cc == 0 else None)
                prev_ev2[par] = b.op("scalar", lambda e, reg=reg, qc=qc, tt=tt,
                                     psb=psb: e.copy(
                    QKI[:, reg + T * qc + 512 * tt:reg + T * qc + 512 * tt + 512],
                    psb[:]), wait=[mm])
    b.barrier()
    prev_ev2 = [None, None]
    for tc in range(32):
        par = tc % 2
        psb = [PS_O4, PS_L4][par]
        mm = None
        for cc in range(4):
            mm = b.op("tensor", lambda e, cc=cc, tc=tc, psb=psb: e.matmul(
                psb[:], CBUF[:, T * cc + 128 * tc:T * cc + 128 * (tc + 1)],
                WS4[:, 4096 + 512 * cc:4096 + 512 * (cc + 1)],
                start=(cc == 0), stop=(cc == 3)),
                wait=[prev_ev2[par]] if cc == 0 else None)
        prev_ev2[par] = b.op("scalar", lambda e, tc=tc, psb=psb: e.copy(
            VBF[:, 512 * tc:512 * (tc + 1)], psb[:]), wait=[mm])
    QSWI = HB4[:, 0:4096]
    fins4 = {}
    g4sw = {"swev": None, "bvlast": None}
    for reg in (0, 4 * T):
        for qc in range(4):
            sw_evs = [None] * 8
            swev = g4sw["swev"]
            for tt in range(8):
                wv = [swev]
                if tt == 0:
                    wv.append(g4sw["bvlast"])
                t_sw = b.op("tensor", lambda e, reg=reg, qc=qc, tt=tt: e.matmul(
                    PS_B4[:], C_swapI[:],
                    QKI[:, reg + T * qc + 512 * tt:reg + T * qc + 512 * tt + 512],
                    start=True, stop=True), wait=wv)
                swev = b.op("scalar", lambda e, tt=tt: e.copy(
                    QSWI[:, 512 * tt:512 * (tt + 1)], PS_B4[:]), wait=[t_sw])
                sw_evs[tt] = swev
            g4sw["swev"] = swev
            fins = rope_combine(QKI, reg + T * qc, QSWI, 0, din["ropeA_i"],
                                din["ropeB_i"], 8, sw_evs, TMPA4, TMPB4)
            fins4[(reg, qc)] = fins
            g4sw["bvlast"] = fins[7]

    last = None
    for g in range(4):
        tmaxs = []
        for (roff, sc, fk) in ((T * g, 8, (0, g)), ((4 + g) * T, 9, (4 * T, g))):
            for tt in range(8):
                t_sq = b.op("scalar", lambda e, roff=roff, tt=tt: e.activation(
                    SQB4[:, 0:512], QKI[:, roff + 512 * tt:roff + 512 * tt + 512],
                    AF.Square), wait=[last, fins4[fk][tt]])
                t_mm = b.op("tensor", lambda e: e.matmul(
                    PS_L4[0:1, :], C_onesB[:, 0:1], SQB4[:, 0:512], start=True,
                    stop=True), wait=[t_sq, last])
                t_mx = b.op("vector", lambda e: e.reduce_max(
                    STAT[0:1, 14:15], PS_L4[0:1, :], axis=AX.X), wait=[t_mm])
                if tt == 0:
                    last = b.op("vector", lambda e, sc=sc: e.tensor_copy(
                        STAT[0:1, sc:sc + 1], STAT[0:1, 14:15]))
                else:
                    last = b.op("vector", lambda e, sc=sc: e.tensor_tensor(
                        STAT[0:1, sc:sc + 1], STAT[0:1, sc:sc + 1],
                        STAT[0:1, 14:15], op=AluOpType.max))
            tmaxs.append(last)
        last = bias_from_norms(PS_S4, ISC_I, 4 + g, tmaxs)
    b.barrier()

    for g in range(4):
        p_guard = [None, None]
        ps_o_guard = None
        ps_l_guard = None
        prev_exp = None
        prev_odma = None
        for qt in range(8):
            nkc = 4 * qt + 4
            last_pv = None
            last_l = None
            tps = {}
            for kc in range(nkc + 1):
                if kc < nkc:
                    psl = kc % 2
                    mm = None
                    for hh in range(4):
                        mm = b.op("tensor", lambda e, g=g, hh=hh, kc=kc, qt=qt: e.matmul(
                            PS_S4[:, 512 * hh:512 * (hh + 1)],
                            QKI[32 * hh:32 * (hh + 1),
                                (4 + g) * T + 128 * kc:(4 + g) * T + 128 * (kc + 1)],
                            QKI[32 * hh:32 * (hh + 1),
                                T * g + 512 * qt:T * g + 512 * qt + 512],
                            start=True, stop=True, tile_position=(32 * hh, 0)),
                            wait=[prev_exp] if hh == 0 else None)
                    t_exp = b.op("scalar", lambda e, g=g, psl=psl: e.activation(
                        PB4[:, 2048 * psl:2048 * (psl + 1)], PS_S4[:], AF.Exp,
                        bias=CBIAS[:, 4 + g:5 + g], scale=float(ISC_I)),
                        wait=[mm, p_guard[psl]])
                    prev_exp = t_exp
                    t_p = t_exp
                    j = kc - 4 * qt
                    if j >= 0:
                        for hh4 in range(4):
                            t_p = b.gop(lambda e, psl=psl, j=j, hh4=hh4: e.tensor_tensor(
                                PB4[:, 2048 * psl + 512 * hh4:2048 * psl + 512 * (hh4 + 1)],
                                PB4[:, 2048 * psl + 512 * hh4:2048 * psl + 512 * (hh4 + 1)],
                                C_masks[:, 512 * j:512 * (j + 1)], op=MULT),
                                wait=[t_exp] if hh4 == 0 else None)
                    tps[kc] = t_p
                pk = kc - 1
                if pk >= 0:
                    psl = pk % 2
                    for hh in range(4):
                        wv2 = [tps[pk]] if hh == 0 else None
                        if pk == 0 and hh == 0:
                            wv2 = [tps[pk], ps_o_guard]
                        last_pv = b.op("tensor", lambda e, g=g, hh=hh, pk=pk, psl=psl,
                                       nkc=nkc: e.matmul(
                            PS_O4[32 * hh:32 * (hh + 1), :],
                            VBF[:, 512 * pk + 128 * g + 32 * hh:
                                512 * pk + 128 * g + 32 * (hh + 1)],
                            PB4[:, 2048 * psl + 512 * hh:2048 * psl + 512 * (hh + 1)],
                            start=(pk == 0), stop=(pk == nkc - 1),
                            tile_position=(0, 32 * hh),
                            skip_group_check=True), wait=wv2)
                    for hh in range(4):
                        wv2 = None
                        if pk == 0 and hh == 0:
                            wv2 = [ps_l_guard]
                        last_l = b.op("tensor", lambda e, hh=hh, pk=pk, psl=psl,
                                      nkc=nkc: e.matmul(
                            PS_L4[32 * hh:32 * hh + 1, :], C_onesB[:, 0:1],
                            PB4[:, 2048 * psl + 512 * hh:2048 * psl + 512 * (hh + 1)],
                            start=(pk == 0), stop=(pk == nkc - 1),
                            tile_position=(0, 32 * hh),
                            skip_group_check=True), wait=wv2)
                    p_guard[psl] = last_l
            t_rec = None
            for hh in range(4):
                t_rec = b.op("vector", lambda e, hh=hh: e.reciprocal(
                    RECROW[32 * hh:32 * hh + 1, :], PS_L4[32 * hh:32 * hh + 1, :]),
                    wait=[last_l] if hh == 0 else None)
            ps_l_guard = t_rec
            t_bc = None
            for hh in range(4):
                t_bc = b.op("tensor", lambda e, hh=hh: e.matmul(
                    PS_B4[32 * hh:32 * (hh + 1), :],
                    C_onesF[32 * hh:32 * hh + 1, 0:32],
                    RECROW[32 * hh:32 * hh + 1, :],
                    start=True, stop=True, tile_position=(32 * hh, 32 * hh)),
                    wait=[t_rec] if hh == 0 else None)
            t_bcev = b.op("scalar", lambda e: e.copy(BCAST[:], PS_B4[:]),
                          wait=[t_bc])
            ps_o_guard = b.op("vector", lambda e: e.tensor_tensor(
                OST4[:], PS_O4[:], BCAST[:], op=MULT),
                wait=[t_bcev, last_pv, prev_odma])
            prev_odma = b.ld(o2_s[g][:, 512 * qt:512 * qt + 512], OST4[:],
                             wait=[ps_o_guard])
        b.barrier()
    b.pop()
    b.pop()

    b.gld(WS4[:, 0:4096], din["wupr"].rearrange("p l m f -> p (l m f)"))
    b.gld(WS4[:, 6144:6208], din["routerr"].rearrange("p c e -> p (c e)"))
    b.barrier()
    b.push()
    OSTR = b.sbuf("p4_ostr", [128, 2048], BF16)
    FPA4 = b.sbuf("p4_fpa", [128, 4096], F32)
    FPB4 = b.sbuf("p4_fpb", [128, 4096], F32)
    FPC4 = b.sbuf("p4_fpc", [128, 4096], F32)
    XR4 = b.sbuf("p4_xr", [128, 4096], F32)
    BQ4 = b.sbuf("p4_bq", [128, 4096], BF16)
    TB4 = b.sbuf("p4_tb", [128, 8192], BF16)
    U4 = b.sbuf("p4_u", [128, 16], F32)
    G4 = b.sbuf("p4_g", [128, 16], BF16)
    M4 = b.sbuf("p4_m", [128, 48], F32)
    PS_M4 = b.psum("p4_m2", [128, 512], F32)
    PS_U4 = b.psum("p4_u2", [128, 1024], F32)
    PS_R4 = b.psum("p4_r", [128, 1024], F32)
    PS_G4 = b.psum("p4_gt", [128, 2048], BF16)

    def stats_rstd4(src_bf, wait):
        mm = None
        for m in range(8):
            mm = b.op("tensor", lambda e, m=m: e.matmul(
                PS_M4[0:1, :], C_onesB[:, 0:1], src_bf[:, 512 * m:512 * (m + 1)],
                start=(m == 0), stop=(m == 7)), wait=wait if m == 0 else None)
        t_srt = b.op("scalar", lambda e: e.activation(
            SROW[0:1, 512:1024], PS_M4[0:1, :], AF.Sqrt, bias=C_eps[0:1, 0:1], scale=1.0 / D),
            wait=[mm])
        t_rc = b.op("vector", lambda e: e.reciprocal(
            SROW[0:1, 1024:1536], SROW[0:1, 512:1024]), wait=[t_srt])
        t_b = b.op("tensor", lambda e: e.matmul(
            PS_M4[:], C_onesF[0:1, :], SROW[0:1, 1024:1536], start=True, stop=True),
            wait=[t_rc])
        return b.op("scalar", lambda e: e.copy(BCAST[:], PS_M4[:]), wait=[t_b])

    rg = {"tu": [None, None], "tg": [None, None], "tgc": [None, None]}

    def emit_router(blk, t_t_blk):
        tbo = 4096 * (blk % 2)
        for tc in range(4):
            sl = tc % 2
            uo, mo, ro, go, gpo = 8 * sl, 24 * sl, 512 * sl, 8 * sl, 1024 * sl
            U = U4[:, uo:uo + 8]
            mm = None
            for dd in range(8):
                wv = None
                if dd == 0:
                    wv = [rg["tu"][sl]]
                    if tc == 0:
                        wv.append(t_t_blk)
                mm = b.op("tensor", lambda e, dd=dd, tc=tc, tbo=tbo, ro=ro: e.matmul(
                    PS_R4[:, ro:ro + 8],
                    TB4[:, tbo + 512 * dd + 128 * tc:tbo + 512 * dd + 128 * (tc + 1)],
                    WS4[:, 6144 + 8 * dd:6144 + 8 * (dd + 1)],
                    start=(dd == 0), stop=(dd == 7)), wait=wv)
            t_u = b.op("scalar", lambda e, U=U, ro=ro: e.copy(U, PS_R4[:, ro:ro + 8]),
                       wait=[mm, rg["tg"][sl]])
            rg["tu"][sl] = t_u
            t_m1 = b.op("vector", lambda e, U=U, mo=mo: e.reduce_max(
                M4[:, mo:mo + 1], U, axis=AX.X), wait=[t_u])
            t_ms1 = b.op("vector", lambda e, mo=mo: e.tensor_scalar(
                M4[:, mo + 1:mo + 2], M4[:, mo:mo + 1], -1.0, None, op0=MULT),
                wait=[t_m1])
            t_k1 = b.op("vector", lambda e, U=U, mo=mo: e.tensor_scalar(
                M4[:, mo + 8:mo + 16], U, M4[:, mo:mo + 1], 1.0,
                op0=AluOpType.is_ge, op1=AluOpType.min), wait=[t_m1])
            t_u2 = b.op("vector", lambda e, U=U, mo=mo: e.scalar_tensor_tensor(
                M4[:, mo + 16:mo + 24], M4[:, mo + 8:mo + 16], -1e30, U,
                op0=MULT, op1=ADD), wait=[t_k1])
            t_m2 = b.op("vector", lambda e, mo=mo: e.reduce_max(
                M4[:, mo + 2:mo + 3], M4[:, mo + 16:mo + 24], axis=AX.X),
                wait=[t_u2])
            t_e = b.op("scalar", lambda e, U=U, mo=mo: e.activation(
                U, U, AF.Exp, bias=M4[:, mo + 1:mo + 2]), wait=[t_m2, t_ms1])
            t_ed = b.op("scalar", lambda e, mo=mo: e.activation(
                M4[:, mo + 3:mo + 4], M4[:, mo + 2:mo + 3], AF.Exp,
                bias=M4[:, mo + 1:mo + 2]))
            t_ed1 = b.op("vector", lambda e, mo=mo: e.tensor_scalar(
                M4[:, mo + 4:mo + 5], M4[:, mo + 3:mo + 4], 1.0, None, op0=ADD),
                wait=[t_ed])
            t_rd = b.op("vector", lambda e, mo=mo: e.reciprocal(
                M4[:, mo + 5:mo + 6], M4[:, mo + 4:mo + 5]), wait=[t_ed1])
            t_k2 = b.op("vector", lambda e, U=U, mo=mo: e.tensor_scalar(
                M4[:, mo + 8:mo + 16], U, M4[:, mo + 3:mo + 4], 1.0,
                op0=AluOpType.is_ge, op1=AluOpType.min), wait=[t_e, t_ed])
            b.op("vector", lambda e, U=U, mo=mo: e.tensor_tensor(
                U, U, M4[:, mo + 8:mo + 16], op=MULT), wait=[t_k2])
            t_g = b.op("vector", lambda e, U=U, mo=mo, go=go: e.tensor_scalar(
                G4[:, go:go + 8], U, M4[:, mo + 5:mo + 6], None, op0=MULT),
                wait=[t_rd])
            rg["tg"][sl] = t_g
            t_tr = b.op("tensor", lambda e, go=go, gpo=gpo: e.transpose(
                PS_G4[0:8, gpo:gpo + 128], G4[:, go:go + 8], C_identB[:]),
                wait=[t_g, rg["tgc"][sl]])
            rg["tgc"][sl] = b.op("scalar", lambda e, blk=blk, tc=tc, gpo=gpo: e.copy(
                GATE[:, 512 * blk + 128 * tc:512 * blk + 128 * (tc + 1)],
                PS_G4[0:8, gpo:gpo + 128]), wait=[t_tr])

    g4g = {"t_xf": None, "lastwup": None, "xfdma": None, "tsdma": [None, None],
           "t_cp": None, "statmm": None, "t_t": [None, None], "umm": [None, None]}
    for blk in range(8):
        tbo = 4096 * (blk % 2)
        t_xr = b.ld_on("dTB0", cpt(XR4[:], 8),
                       ln3_s[:, :, 512 * blk:512 * blk + 512]
                       .rearrange("c p t -> p c t"), wait=[g4g["t_xf"]])
        t_os = b.ld_on("dTB1", cpt(OSTR[:], 4),
                       o2_s[:, :, 512 * blk:512 * blk + 512]
                       .rearrange("g p t -> p g t"), wait=[g4g["lastwup"]])
        prev_ev2 = [None, None]
        mm = None
        for m in range(8):
            par = m % 2
            mm = None
            for lc in range(4):
                wv = None
                if lc == 0:
                    wv = [prev_ev2[par], t_os]
                mm = b.op("tensor", lambda e, m=m, lc=lc, par=par: e.matmul(
                    PS_U4[:, 512 * par:512 * (par + 1)],
                    WS4[:, 1024 * lc + 128 * m:1024 * lc + 128 * (m + 1)],
                    OSTR[:, 512 * lc:512 * (lc + 1)],
                    start=(lc == 0), stop=(lc == 3)), wait=wv)
            wv = [mm]
            if m == 0:
                wv.append(g4g["t_xf"])
            prev_ev2[par] = b.op("scalar", lambda e, m=m, par=par: e.copy(
                FPA4[:, 512 * m:512 * (m + 1)],
                PS_U4[:, 512 * par:512 * (par + 1)]), wait=wv)
        g4g["lastwup"] = mm
        t_xf = None
        for m in range(8):
            wv = None
            if m == 0:
                wv = [prev_ev2[0], prev_ev2[1], t_xr, g4g["xfdma"],
                      g4g["t_cp"]]
            t_xf = b.op("vector", lambda e, m=m: e.tensor_tensor(
                FPB4[:, 512 * m:512 * (m + 1)], FPA4[:, 512 * m:512 * (m + 1)],
                XR4[:, 512 * m:512 * (m + 1)], op=ADD), wait=wv)
        g4g["t_xf"] = t_xf
        g4g["xfdma"] = b.gld_on(
            "gSA", xf_s[:, :, 512 * blk:512 * blk + 512]
            .rearrange("c p t -> p c t"), cpt(FPB4[:], 8), wait=[t_xf])
        t_cp = None
        for m in range(8):
            wv = None
            if m == 0:
                wv = [t_xf, g4g["statmm"]]
            t_cp = b.op("scalar", lambda e, m=m: e.copy(
                BQ4[:, 512 * m:512 * (m + 1)], FPB4[:, 512 * m:512 * (m + 1)]),
                wait=wv)
        g4g["t_cp"] = t_cp
        t_sq2 = None
        for m in range(8):
            t_sq2 = b.op("scalar", lambda e, m=m: e.activation(
                BQ4[:, 512 * m:512 * (m + 1)], BQ4[:, 512 * m:512 * (m + 1)],
                AF.Square), wait=[t_cp] if m == 0 else None)
        t_rb = stats_rstd4(BQ4, [t_sq2])
        g4g["statmm"] = t_rb
        t_t = None
        sl = blk % 2
        for m in range(8):
            wv = None
            if m == 0:
                wv = [t_rb, g4g["tsdma"][sl], g4g["umm"][sl]]
            b.op("vector", lambda e, m=m: e.tensor_tensor(
                FPC4[:, 512 * m:512 * (m + 1)], FPB4[:, 512 * m:512 * (m + 1)],
                BCAST[:], op=MULT), wait=wv)
            t_t = b.op("vector", lambda e, m=m, tbo=tbo: e.tensor_scalar(
                TB4[:, tbo + 512 * m:tbo + 512 * (m + 1)],
                FPC4[:, 512 * m:512 * (m + 1)],
                C_g2[:, m:m + 1], None, op0=MULT))
        g4g["t_t"][sl] = t_t
        g4g["tsdma"][sl] = b.gld_on(
            "gSB", t_s[:, :, 512 * blk:512 * blk + 512]
            .rearrange("c p t -> p c t"),
            cpt(TB4[:, tbo:tbo + 4096], 8), wait=[t_t])
        if blk >= 1:
            emit_router(blk - 1, g4g["t_t"][(blk - 1) % 2])
            g4g["umm"][(blk - 1) % 2] = rg["tu"][1]
    emit_router(7, g4g["t_t"][1])
    b.barrier()
    b.pop()
    b.pop()
    b.pop()
    if "gate_s" in taps:
        b.ld(gate_s[:, :], GATE[:, :])
        b.barrier()

    if upto < 5:
        return finish(nc, st, b)

    b.push()
    Y5 = b.sbuf("p5_y", [128, 8192], F32)
    TS5 = b.sbuf("p5_t", [128, 8192], BF16)
    W15 = b.sbuf("p5_w1", [128, 16384], BF16)
    W35 = b.sbuf("p5_w3", [128, 16384], BF16)
    W25 = b.sbuf("p5_w2", [128, 16384], BF16)

    def issue_w13_chunk(ee, fc4, wait):
        b.gld_on("gW13", W15[:, 4096 * fc4:4096 * (fc4 + 1)]
                 .rearrange("p (fc x) -> p fc x", fc=4),
                 din["w1r"][ee, 4 * fc4:4 * (fc4 + 1)]
                 .rearrange("fc p d f -> p fc (d f)"),
                 wait=wait)
        return b.gld_on("gW13", W35[:, 4096 * fc4:4096 * (fc4 + 1)]
                        .rearrange("p (fc x) -> p fc x", fc=4),
                        din["w3r"][ee, 4 * fc4:4 * (fc4 + 1)]
                        .rearrange("fc p d f -> p fc (d f)"))

    def issue_w2(ee, wait):
        t = None
        for fc4 in range(4):
            t = b.gld_on("gW2", W25[:, 4096 * fc4:4096 * (fc4 + 1)]
                         .rearrange("p (fc m) -> p fc m", fc=4),
                         din["w2r"][ee][:, 4 * fc4:4 * (fc4 + 1), :],
                         wait=wait if fc4 == 0 else None)
        return t

    pf = {"w13ct": None, "w2ct": None, "lastA": None}

    def emit_super(s):
        t_ts = b.ld_on("gTS", cpt(TS5[:], 8),
                       t_s[:, :, 1024 * s:1024 * (s + 1)]
                       .rearrange("c p t -> p c t"), wait=[pf["lastA"]])
        b.barrier(skip_gw=True)
        b.push()
        HID5 = b.sbuf(f"p5_hid{s}", [128, 8192], BF16)
        GB5 = b.sbuf(f"p5_gb{s}", [128, 1024], BF16)
        SIL5 = b.sbuf(f"p5_sil{s}", [128, 1024], F32)
        TMP5 = b.sbuf(f"p5_tmp{s}", [128, 1024], F32)
        PS_A1 = b.psum(f"p5_a1{s}", [128, 1024], F32)
        PS_A3p = b.psum(f"p5_a3{s}", [128, 1024], F32)
        PS_G5 = b.psum(f"p5_g{s}", [128, 512], F32)
        PS_Y5 = b.psum(f"p5_yp{s}", [128, 1024], F32)

        b.op("vector", lambda e: e.memset(Y5[:], 0.0))

        if pf["w13ct"] is None:
            w13_ct = [issue_w13_chunk(0, fc4, None) for fc4 in range(4)]
            w2_tgt = issue_w2(0, None)
        else:
            w13_ct, w2_tgt = pf["w13ct"], pf["w2ct"]
            pf["w13ct"] = pf["w2ct"] = None
        sil_g = [None, None]
        tmp_g = [None, None]
        y_g = [None, None]
        hid_done = [None, None]
        B_done = [None, None]
        prev_hid_final = None
        t_gbe = None
        lastA = None
        lastB = None
        for ee in range(8):
            for tb in range(2):
                t_gb = b.op("tensor", lambda e, ee=ee, s=s, tb=tb: e.matmul(
                    PS_G5[:], C_epat[:, 128 * ee:128 * (ee + 1)],
                    GATE[:, 1024 * s + 512 * tb:1024 * s + 512 * tb + 512],
                    start=True, stop=True), wait=[t_gbe])
                t_gbe = b.op("scalar", lambda e, tb=tb: e.copy(
                    GB5[:, 512 * tb:512 * (tb + 1)], PS_G5[:]),
                    wait=[t_gb, prev_hid_final])
            nxt_w13 = []
            has_next = (ee < 7) or (s < 3)
            nee = (ee + 1) % 8
            for tb in range(2):
                for fc in range(16):
                    par = fc % 2
                    mm1 = None
                    for dd in range(8):
                        wv = None
                        if dd == 0:
                            wv = [sil_g[par]]
                            if tb == 0 and fc % 4 == 0:
                                wv.append((b.sem["gW13"], w13_ct[fc // 4][1]))
                            if tb == 0 and fc == 0:
                                wv.append(t_ts)
                        mm1 = b.op("tensor", lambda e, fc=fc, dd=dd, tb=tb,
                                   par=par: e.matmul(
                            PS_A1[:, 512 * par:512 * (par + 1)],
                            W15[:, 1024 * fc + 128 * dd:1024 * fc + 128 * (dd + 1)],
                            TS5[:, 1024 * dd + 512 * tb:1024 * dd + 512 * tb + 512],
                            start=(dd == 0), stop=(dd == 7)), wait=wv)
                    mm3 = None
                    for dd in range(8):
                        mm3 = b.op("tensor", lambda e, fc=fc, dd=dd, tb=tb,
                                   par=par: e.matmul(
                            PS_A3p[:, 512 * par:512 * (par + 1)],
                            W35[:, 1024 * fc + 128 * dd:1024 * fc + 128 * (dd + 1)],
                            TS5[:, 1024 * dd + 512 * tb:1024 * dd + 512 * tb + 512],
                            start=(dd == 0), stop=(dd == 7)),
                            wait=[tmp_g[par]] if dd == 0 else None)
                    lastA = mm3
                    if tb == 1 and fc % 4 == 3 and has_next:
                        nxt_w13.append(issue_w13_chunk(nee, fc // 4, [mm3]))
                    t_sil = b.op("scalar", lambda e, par=par: e.activation(
                        SIL5[:, 512 * par:512 * (par + 1)],
                        PS_A1[:, 512 * par:512 * (par + 1)], AF.Silu),
                        wait=[mm1])
                    wv = [mm3]
                    if tb == 0 and fc == 0:
                        wv.append(t_gbe)
                    t_tmp = b.op("vector", lambda e, tb=tb, par=par: e.tensor_tensor(
                        TMP5[:, 512 * par:512 * (par + 1)],
                        PS_A3p[:, 512 * par:512 * (par + 1)],
                        GB5[:, 512 * tb:512 * (tb + 1)], op=MULT), wait=wv)
                    wv = [t_sil]
                    if fc == 0:
                        wv.append(B_done[tb])
                    t_hid = b.op("vector", lambda e, fc=fc, par=par:
                                 e.tensor_tensor(
                        HID5[:, 512 * fc:512 * (fc + 1)],
                        TMP5[:, 512 * par:512 * (par + 1)],
                        SIL5[:, 512 * par:512 * (par + 1)], op=MULT), wait=wv)
                    sil_g[par] = t_hid
                    tmp_g[par] = t_hid
                hid_done[tb] = t_hid
                prev_hid_final = t_hid
                lastBm = None
                for m in range(8):
                    ypar = m % 2
                    mm = None
                    for fc in range(16):
                        wv = None
                        if fc == 0:
                            wv = [hid_done[tb], y_g[ypar]]
                            if tb == 0 and m == 0:
                                wv.append((b.sem["gW2"], w2_tgt[1]))
                        mm = b.op("tensor", lambda e, m=m, fc=fc, ypar=ypar:
                                  e.matmul(
                            PS_Y5[:, 512 * ypar:512 * (ypar + 1)],
                            W25[:, 1024 * fc + 128 * m:1024 * fc + 128 * (m + 1)],
                            HID5[:, 512 * fc:512 * (fc + 1)],
                            start=(fc == 0), stop=(fc == 15)), wait=wv)
                    y_g[ypar] = b.op("vector", lambda e, m=m, tb=tb, ypar=ypar:
                                     e.tensor_tensor(
                        Y5[:, 1024 * m + 512 * tb:1024 * m + 512 * tb + 512],
                        Y5[:, 1024 * m + 512 * tb:1024 * m + 512 * tb + 512],
                        PS_Y5[:, 512 * ypar:512 * (ypar + 1)], op=ADD),
                        wait=[mm])
                    lastBm = mm
                B_done[tb] = lastBm
                lastB = lastBm
            if has_next:
                w13_ct = nxt_w13
                w2_tgt = issue_w2(nee, [lastB])
        if s < 3:
            pf["w13ct"], pf["w2ct"] = w13_ct, w2_tgt
        pf["lastA"] = lastA
        b.barrier(skip_gw=True)

        if "y_s" in taps:
            for c in range(8):
                b.ld(y_s[c, :, 1024 * s:1024 * (s + 1)],
                     Y5[:, 1024 * c:1024 * (c + 1)])
            b.barrier()

        b.pop()
        b.push()
        X16 = b.sbuf(f"p6_x1{s}", [128, 2048], F32)
        OF6 = b.sbuf(f"p6_of{s}", [128, 4096], F32)
        BQ6 = b.sbuf(f"p6_bq{s}", [128, 4096], BF16)
        OS6 = b.sbuf(f"p6_os{s}", [128, 2048], F32)
        PS_M6 = b.psum(f"p6_m{s}", [128, 512], F32)
        PS_B6 = b.psum(f"p6_b{s}", [128, 512], F32)
        PS_T6 = b.psum(f"p6_t{s}", [128, 1024], F32)

        def stats6(src_bf, kind, wait):
            mm = None
            for m in range(8):
                mm = b.op("tensor", lambda e, m=m: e.matmul(
                    PS_M6[0:1, :], C_onesB[:, 0:1],
                    src_bf[:, 512 * m:512 * (m + 1)],
                    start=(m == 0), stop=(m == 7)), wait=wait if m == 0 else None)
            if kind == "mean":
                t_r = b.op("vector", lambda e: e.tensor_scalar(
                    SROW[0:1, 512:1024], PS_M6[0:1, :], 1.0 / D, None, op0=MULT),
                    wait=[mm])
                t_b = b.op("tensor", lambda e: e.matmul(
                    PS_B6[:], C_onesF[0:1, :], SROW[0:1, 512:1024],
                    start=True, stop=True), wait=[t_r])
            else:
                t_srt = b.op("scalar", lambda e: e.activation(
                    SROW[0:1, 512:1024], PS_M6[0:1, :], AF.Sqrt,
                    bias=C_eps[0:1, 0:1], scale=1.0 / D), wait=[mm])
                t_rc = b.op("vector", lambda e: e.reciprocal(
                    SROW[0:1, 1024:1536], SROW[0:1, 512:1024]), wait=[t_srt])
                t_b = b.op("tensor", lambda e: e.matmul(
                    PS_B6[:], C_onesF[0:1, :], SROW[0:1, 1024:1536],
                    start=True, stop=True), wait=[t_rc])
            return b.op("scalar", lambda e: e.copy(BCAST[:], PS_B6[:]),
                        wait=[t_b])

        ep = {"osdma": [None, None], "tp": None, "x2": None, "x2h": None}
        for tb in range(2):
            blk = 2 * s + tb
            t_xf = b.ld_on("dTA1", cpt(OF6[:], 8),
                           xf_s[:, :, 512 * blk:512 * blk + 512]
                           .rearrange("c p t -> p c t"),
                           wait=[ep["tp"]])
            t_x1a = b.ld_on(
                "dTA0", cpt(X16[:], 4),
                x1_s[0:4, :, 512 * blk:512 * blk + 512]
                .rearrange("c p t -> p c t"), wait=[ep["x2h"]])
            t_of = None
            for m in range(8):
                t_of = b.op("vector", lambda e, m=m, tb=tb: e.tensor_tensor(
                    OF6[:, 512 * m:512 * (m + 1)], OF6[:, 512 * m:512 * (m + 1)],
                    Y5[:, 1024 * m + 512 * tb:1024 * m + 512 * tb + 512], op=ADD),
                    wait=[t_xf] if m == 0 else None)
            t_cp = None
            for m in range(8):
                t_cp = b.op("scalar", lambda e, m=m: e.copy(
                    BQ6[:, 512 * m:512 * (m + 1)], OF6[:, 512 * m:512 * (m + 1)]),
                    wait=[t_of] if m == 0 else None)
            t_mb = stats6(BQ6, "mean", [t_cp])
            t_c = None
            for m in range(8):
                t_c = b.op("vector", lambda e, m=m: e.tensor_tensor(
                    OF6[:, 512 * m:512 * (m + 1)], OF6[:, 512 * m:512 * (m + 1)],
                    BCAST[:], op=SUB), wait=[t_mb] if m == 0 else None)
            t_sq = None
            for m in range(8):
                t_sq = b.op("scalar", lambda e, m=m: e.activation(
                    BQ6[:, 512 * m:512 * (m + 1)], OF6[:, 512 * m:512 * (m + 1)],
                    AF.Square), wait=[t_c] if m == 0 else None)
            t_rb = stats6(BQ6, "rstd", [t_sq])
            t_x2 = None
            t_x1b = None
            for m in range(8):
                b.op("vector", lambda e, m=m: e.tensor_tensor(
                    OF6[:, 512 * m:512 * (m + 1)], OF6[:, 512 * m:512 * (m + 1)],
                    BCAST[:], op=MULT), wait=[t_rb] if m == 0 else None)
                wv = None
                if m == 0:
                    wv = [t_x1a]
                elif m == 4:
                    wv = [t_x1b]
                t_x2 = b.op("vector", lambda e, m=m: e.tensor_tensor(
                    OF6[:, 512 * m:512 * (m + 1)], OF6[:, 512 * m:512 * (m + 1)],
                    X16[:, 512 * (m % 4):512 * (m % 4 + 1)], op=ADD), wait=wv)
                if m == 3:
                    t_x1b = b.ld_on(
                        "dTA0", cpt(X16[:], 4),
                        x1_s[4:8, :, 512 * blk:512 * blk + 512]
                        .rearrange("c p t -> p c t"), wait=[t_x2])
            ep["x2h"] = t_x2
            tp = None
            for tc in range(4):
                osl = tc % 2
                tpl = None
                for c in range(8):
                    wv = None
                    if c == 0:
                        wv = [t_x2] if tc == 0 else [ep["tp"]]
                        wv = [t_x2, ep["tp"]] if tc == 0 else [ep["tp"]]
                    tpl = b.op("tensor", lambda e, c=c, tc=tc: e.transpose(
                        PS_T6[:, 128 * c:128 * (c + 1)],
                        OF6[:, 512 * c + 128 * tc:512 * c + 128 * (tc + 1)],
                        C_identF[:]), wait=wv)
                t_ev = b.op("scalar", lambda e, osl=osl: e.copy(
                    OS6[:, 1024 * osl:1024 * (osl + 1)], PS_T6[:]),
                    wait=[tpl, ep["osdma"][osl]])
                ep["tp"] = t_ev
                ep["osdma"][osl] = b.ld_on(
                    f"dTB{osl}",
                    outh[512 * blk + 128 * tc:512 * blk + 128 * (tc + 1), :],
                    OS6[:, 1024 * osl:1024 * (osl + 1)], wait=[t_ev])
        b.barrier(skip_gw=True)
        b.pop()

    for s in range(4):
        emit_super(s)
    b.pop()
    return finish(nc, st, b)


def finish(nc, st, b):
    b.barrier()
    with nc.Block() as block:
        @block.tensor
        def _(e):
            for f in b.ops["tensor"]:
                f(e)

        @block.scalar
        def _(e):
            for f in b.ops["scalar"]:
                f(e)

        @block.vector
        def _(e):
            for f in b.ops["vector"]:
                f(e)

        @block.gpsimd
        def _(e):
            for f in b.ops["gpsimd"]:
                f(e)

        @block.sync
        def _(e):
            for f in b.ops["sync"]:
                f(e)

    while b.stacks:
        b.pop()
    st.close()
    return nc


def kernel(**inputs):
    x = np.ascontiguousarray(inputs["L0_attn_x"])
    consts = host_constants()
    wr = host_weights({k: np.ascontiguousarray(v) for k, v in inputs.items()})
    nc = build_program(taps=tuple(DEBUG_TAPS))
    in_maps = []
    for bch in range(8):
        m = {"x": np.ascontiguousarray(x[bch])}
        m.update(consts)
        m.update(wr)
        in_maps.append(m)
    res = run_bass_kernel_spmd(nc, in_maps, core_ids=list(range(8)))
    out = np.stack([np.asarray(res.results[i]["out"]) for i in range(8)], axis=0)
    return out.astype(np.float32)
